# version 04
# baseline (speedup 1.0000x reference)
import math
import numpy as np
import ml_dtypes
from contextlib import ExitStack

import concourse.bass as bass
import concourse.mybir as mybir

F32 = mybir.dt.float32
BF16 = mybir.dt.bfloat16
AF = mybir.ActivationFunctionType
ALU = mybir.AluOpType
AX = mybir.AxisListType

D = 1024
T = 4096
NKC = D // 128
NTT = T // 512
LN_EPS = 1e-5


class E:
    NO_TRACK = {"wait_ge", "nop"}

    def __init__(self, eng, sem, step, name):
        self.eng = eng
        self.sem = sem
        self.step = step
        self.name = name
        self.cnt = 0
        self.last = None
        self.dirty = False
        self.hw = {}

    def __getattr__(self, name):
        fn = getattr(self.eng, name)
        if not callable(fn) or name in E.NO_TRACK:
            return fn
        def wrap(*a, **k):
            inst = fn(*a, **k)
            self.last = inst
            if name.startswith("dma_start"):
                inst.then_inc(self.sem, self.step)
                self.cnt += self.step
                self.dirty = False
            else:
                self.dirty = True
            return inst
        return wrap

    def inc(self):
        if self.dirty:
            self.last.then_inc(self.sem, self.step)
            self.cnt += self.step
            self.dirty = False
        return self.cnt

    def ss(self):
        tok = self.inc()
        if tok > 0:
            self.eng.wait_ge(self.sem, tok)
        return tok

    def wait(self, other, val=None):
        if val is None:
            val = other.inc()
        if val <= 0:
            return
        key = id(other)
        if self.hw.get(key, -1) >= val:
            return
        self.eng.wait_ge(other.sem, val)
        self.hw[key] = val


class MK:
    def __init__(self, nc: bass.Bass, es: ExitStack):
        self.nc = nc
        self.es = es
        sem = lambda n: es.enter_context(nc.semaphore(n))
        self.T = E(nc.tensor, sem("sT"), 1, "T")
        self.V = E(nc.vector, sem("sV"), 1, "V")
        self.S = E(nc.scalar, sem("sS"), 1, "S")
        self.G = E(nc.gpsimd, sem("sG"), 1, "G")
        self.GD = E(nc.gpsimd, sem("sGD"), 16, "GD")
        self.YD = E(nc.sync, sem("sYD"), 16, "YD")
        self.all_ch = [self.T, self.V, self.S, self.G, self.GD, self.YD]
        self.compute_ch = [self.T, self.V, self.S, self.G]
        self._consts = {}
        self._nm = 0

    def name(self, pfx):
        self._nm += 1
        return f"{pfx}{self._nm}"

    def sb(self, name, shape, dtype, es=None, side=None):
        kw = {"side": side} if side else {}
        return (es or self.es).enter_context(self.nc.sbuf_tensor(name, shape, dtype, **kw))

    def ps(self, name, shape, dtype=F32):
        return self.es.enter_context(self.nc.psum_tensor(name, shape, dtype))

    def barrier(self):
        toks = [(ch, ch.inc()) for ch in self.all_ch]
        for ch in self.compute_ch + [self.YD]:
            for o, v in toks:
                if o is not ch:
                    ch.wait(o, v)

    def init_consts(self):
        G, V = self.G, self.V
        self.ones_col = self.sb("c_ones_col", [128, 1], BF16, side="right")
        self.ones_col_f = self.sb("c_ones_col_f", [128, 1], F32, side="right")
        self.ones_row = self.sb("c_ones_row", [1, 128], BF16, side="right")
        self.ones_row_f = self.sb("c_ones_row_f", [1, 128], F32, side="right")
        self.iota_col = self.sb("c_iota_col", [128, 1], F32, side="right")
        self.ident = self.sb("c_ident", [128, 128], BF16, side="right")
        self.ident_f = self.sb("c_ident_f", [128, 128], F32, side="right")
        G.memset(self.ones_col[:], 1.0)
        G.memset(self.ones_col_f[:], 1.0)
        G.memset(self.ones_row[:], 1.0)
        G.memset(self.ones_row_f[:], 1.0)
        G.iota(self.iota_col[:], pattern=[[0, 1]], base=0, channel_multiplier=1,
               allow_small_or_imprecise_dtypes=True)
        ramp = self.sb("c_ramp512", [128, 512], F32, side="right")
        G.iota(ramp[:], pattern=[[1, 512]], base=0, channel_multiplier=0,
               allow_small_or_imprecise_dtypes=True)
        self.ramp512 = ramp
        gt = G.inc()
        V.wait(G, gt)
        V.tensor_scalar(self.ident[:], ramp[:, 0:128], self.iota_col[:], None, ALU.is_equal)
        V.tensor_scalar(self.ident_f[:], ramp[:, 0:128], self.iota_col[:], None, ALU.is_equal)
        self.cmask = {}
        for off in (0, 128, 256, 384):
            m = self.sb(f"c_mask{off}", [128, 512], BF16, side="right")
            V.tensor_scalar(m[:], ramp[:], float(off), self.iota_col[:], ALU.subtract, ALU.is_ge)
            self.cmask[off] = m
        V.inc()
        self.pA = self.ps("pA", [128, 512])
        self.pB = self.ps("pB", [128, 512])
        self.pC = self.ps("pC", [128, 512])
        self.pD = self.ps("pD", [128, 512])
        self.pE = self.ps("pE", [128, 512])
        self.pF = self.ps("pF", [128, 512])
        self.pG = self.ps("pG", [128, 512])
        self.pT = self.ps("pTbf", [128, 512], BF16)

    def ccol(self, value):
        v = float(value)
        if v in self._consts:
            return self._consts[v]
        t = self.sb(f"c_col{len(self._consts)}", [128, 1], F32, side="right")
        self.G.memset(t[:], v)
        tok = self.G.inc()
        for ch in (self.T, self.V, self.S):
            ch.wait(self.G, tok)
        self._consts[v] = t
        return t


def ln_stats(mk, xT_in, murow, rsrow, xbuf, sqbuf, tmprow):
    S, V, Td, GD = mk.S, mk.V, mk.T, mk.GD
    epsc = mk.ccol(LN_EPS)
    half_tok = {}
    for tt in range(NTT):
        sl = slice(tt * 512, (tt + 1) * 512)
        for kc in range(NKC):
            h = kc % 2
            buf = xbuf[:, h * 512:h * 512 + 512]
            sq = sqbuf[:, h * 512:h * 512 + 512]
            if (tt, kc) != (0, 0) and half_tok.get(h):
                st_, tt_ = half_tok[h]
                GD.wait(S, st_)
                GD.wait(Td, tt_)
            GD.dma_start(out=buf, in_=xT_in[kc * 128:(kc + 1) * 128, sl])
            dt = GD.inc()
            S.wait(GD, dt)
            S.activation(sq, buf, AF.Square)
            st = S.inc()
            Td.wait(S, st)
            Td.matmul(mk.pE[0:1, :], mk.ones_col_f[:, :], buf, start=(kc == 0), stop=(kc == NKC - 1))
            Td.matmul(mk.pF[0:1, :], mk.ones_col_f[:, :], sq, start=(kc == 0), stop=(kc == NKC - 1))
            tk = Td.inc()
            half_tok[h] = (st, tk)
        tmm = Td.inc()
        S.wait(Td, tmm)
        S.activation(murow[0:1, sl], mk.pE[0:1, :], AF.Copy, scale=1.0 / D)
        S.activation(rsrow[0:1, sl], mk.pF[0:1, :], AF.Copy, scale=1.0 / D)
        s_ev = S.inc()
        Td.wait(S, s_ev)
        V.wait(S, s_ev)
        V.tensor_tensor(tmprow[0:1, :], murow[0:1, sl], murow[0:1, sl], ALU.mult)
        V.ss()
        V.tensor_tensor(rsrow[0:1, sl], rsrow[0:1, sl], tmprow[0:1, :], ALU.subtract)
        v_var = V.inc()
        S.wait(V, v_var)
        S.activation(rsrow[0:1, sl], rsrow[0:1, sl], AF.Ln, bias=epsc[0:1, :])
        S.ss()
        S.activation(rsrow[0:1, sl], rsrow[0:1, sl], AF.Exp, scale=-0.5)
        S.inc()


def ln_bcast(mk, mu_ap, rs_ap):
    mk.T.wait(mk.V)
    mk.T.wait(mk.S)
    mk.T.matmul(mk.pE[:, :], mk.ones_row_f[0:1, :], mu_ap, start=True, stop=True)
    mk.T.matmul(mk.pF[:, :], mk.ones_row_f[0:1, :], rs_ap, start=True, stop=True)
    return mk.T.inc()


def load_weight(mk, dst_sb, src_ap, rows, cols, row0=0, col0=0, dst_off=0):
    mk.GD.dma_start(out=dst_sb[0:rows, dst_off:dst_off + cols],
                    in_=src_ap[row0:row0 + rows, col0:col0 + cols])
    return mk.GD.cnt


def load_weight_T(mk, dst_sb, src_ap, R, C, stage):
    if R < 128:
        mk.GD.wait(mk.T)
        load_weight(mk, stage, src_ap, R, C)
        t = mk.GD.inc()
        mk.T.wait(mk.GD, t)
        for cc in range(C // 128):
            mk.T.wait(mk.V)
            mk.T.transpose(mk.pT[0:128, 0:R], stage[0:R, cc * 128:(cc + 1) * 128],
                           mk.ident[0:R, 0:R])
            tt = mk.T.inc()
            mk.V.wait(mk.T, tt)
            mk.V.tensor_copy(dst_sb[:, cc * R:(cc + 1) * R], mk.pT[0:128, 0:R])
            mk.V.inc()
        mk.T.wait(mk.V)
        return
    nr, ncc = R // 128, C // 128
    for rc in range(nr):
        mk.GD.wait(mk.T)
        load_weight(mk, stage, src_ap, 128, C, row0=rc * 128)
        t = mk.GD.inc()
        mk.T.wait(mk.GD, t)
        for cc in range(ncc):
            mk.T.wait(mk.V)
            mk.T.transpose(mk.pT[:, 0:128], stage[0:128, cc * 128:(cc + 1) * 128], mk.ident[:])
            tt = mk.T.inc()
            mk.V.wait(mk.T, tt)
            mk.V.tensor_copy(dst_sb[:, cc * R + rc * 128: cc * R + (rc + 1) * 128],
                             mk.pT[:, 0:128])
            mk.V.inc()
    mk.T.wait(mk.V)


def ln_apply_tile(mk, xT_in, murow, rsrow, tt, lnbuf, W, wofs, xbuf, tmp, prev_tok=None,
                  mu_ap=None, rs_ap=None):
    S, V, Td, GD = mk.S, mk.V, mk.T, mk.GD
    sl = slice(tt * 512, (tt + 1) * 512)
    if mu_ap is None:
        mu_ap = murow[0:1, sl]
        rs_ap = rsrow[0:1, sl]
    bt = ln_bcast(mk, mu_ap, rs_ap)
    for kc in range(NKC):
        h = kc % 2
        buf = xbuf[:, h * 512:h * 512 + 512]
        if prev_tok is not None and prev_tok.get(("lnx", h)):
            GD.wait(V, prev_tok[("lnx", h)])
        GD.dma_start(out=buf, in_=xT_in[kc * 128:(kc + 1) * 128, sl])
        dt = GD.inc()
        V.wait(GD, dt)
        V.wait(Td, bt)
        V.tensor_tensor(tmp[:, :], buf, mk.pE[:, :], ALU.subtract)
        V.tensor_tensor(lnbuf[:, kc * W + wofs: kc * W + wofs + 512], tmp[:, :], mk.pF[:, :], ALU.mult)
        vt = V.inc()
        if prev_tok is not None:
            prev_tok[("lnx", h)] = vt
    return V.inc()


def emit_attn0(mk, xT_in, xT_out, w_attn, w_proj, qscr, vscr, kres, les):
    nc = mk.nc
    S, V, Td, G, GD, YD = mk.S, mk.V, mk.T, mk.G, mk.GD, mk.YD
    H = 16
    HD = 64
    murow = mk.sb(mk.name("murow"), [1, T], F32, les)
    rsrow = mk.sb(mk.name("rsrow"), [1, T], F32, les)
    xbuf = mk.sb(mk.name("xbuf"), [128, 1024], F32, les)
    sqbuf = mk.sb(mk.name("sqbuf"), [128, 1024], F32, les)
    tmprow = mk.sb(mk.name("tmprow"), [1, 512], F32, les)
    tmp512 = mk.sb(mk.name("tmp512"), [128, 512], F32, les)

    ln_stats(mk, xT_in, murow, rsrow, xbuf, sqbuf, tmprow)

    W = 2048
    lnbuf = mk.sb(mk.name("lnbuf"), [128, NKC * W], BF16, les)
    wbuf = mk.sb(mk.name("wbuf"), [128, 2 * 6 * 1024], BF16, les)
    qstage = mk.sb(mk.name("qstage"), [128, W], BF16, les)
    vstage = mk.sb(mk.name("vstage"), [128, W], BF16, les)
    vt_tile = mk.sb(mk.name("vt_tile"), [128, 128], BF16, les)

    vscr4 = vscr.rearrange("(h p k f) -> h p k f", h=16, p=128, k=32)

    vst_tok = [0]

    def wslice(g, m_in_g, kc):
        return wbuf[:, (g % 2) * 6144 + (m_in_g * 8 + kc) * 128:(g % 2) * 6144 + (m_in_g * 8 + kc) * 128 + 128]

    wgrp_tok = {}
    for half in range(2):
        V.wait(Td)
        ptok = {}
        for i, tt in enumerate(range(half * 4, half * 4 + 4)):
            ln_apply_tile(mk, xT_in, murow, rsrow, tt, lnbuf, W, i * 512, xbuf, tmp512, ptok)
        ln_tok = V.inc()
        for g in range(4):
            if wgrp_tok.get(g % 2):
                GD.wait(Td, wgrp_tok[g % 2])
            if half == 0 or True:
                for m_in_g in range(6):
                    m = g * 6 + m_in_g
                    for kc in range(NKC):
                        GD.dma_start(out=wslice(g, m_in_g, kc),
                                     in_=w_attn[kc * 128:(kc + 1) * 128, m * 128:(m + 1) * 128])
                wt = GD.inc()
            for m_in_g in range(6):
                m = g * 6 + m_in_g
                for t2 in range(4):
                    ps = mk.pC if (t2 % 2 == 0) else mk.pD
                    Td.wait(GD, wt)
                    Td.wait(V, ln_tok)
                    Td.wait(S)
                    for kc in range(NKC):
                        Td.matmul(ps[:, :], wslice(g, m_in_g, kc),
                                  lnbuf[:, kc * W + t2 * 512: kc * W + t2 * 512 + 512],
                                  start=(kc == 0), stop=(kc == NKC - 1))
                    mt = Td.inc()
                    S.wait(Td, mt)
                    if m < 8:
                        S.activation(qstage[:, t2 * 512:(t2 + 1) * 512], ps[:, :], AF.Copy, scale=0.125)
                    elif m < 16:
                        S.activation(kres[m - 8][:, half * W + t2 * 512: half * W + (t2 + 1) * 512], ps[:, :], AF.Copy)
                    else:
                        S.wait(Td, vst_tok[0])
                        S.activation(vstage[:, t2 * 512:(t2 + 1) * 512], ps[:, :], AF.Copy)
                    et = S.inc()
                if m < 8:
                    YD.wait(S, et)
                    YD.dma_start(out=qscr[m * 128:(m + 1) * 128, half * W:(half + 1) * W], in_=qstage[:, :])
                    qd = YD.inc()
                    S.wait(YD, qd)
                elif m >= 16:
                    c = m - 16
                    for tc in range(16):
                        kchunk = half * 16 + tc
                        Td.wait(S, et)
                        Td.wait(V)
                        Td.transpose(mk.pT[:, 0:128], vstage[:, tc * 128:(tc + 1) * 128], mk.ident[:])
                        ttr = Td.inc()
                        V.wait(Td, ttr)
                        V.tensor_copy(vt_tile[:, :], mk.pT[:, 0:128])
                        vtt = V.inc()
                        GD.wait(V, vtt)
                        GD.dma_start(out=vscr4[2 * c, :, kchunk, :], in_=vt_tile[:, 0:64])
                        GD.dma_start(out=vscr4[2 * c + 1, :, kchunk, :], in_=vt_tile[:, 64:128])
                        gdt = GD.inc()
                        V.wait(GD, gdt)
                    vst_tok[0] = Td.inc()
            wgrp_tok[g % 2] = Td.inc()
    mk.barrier()
    return dict(vscr4=vscr4)


def emit_attn0_attention(mk, xT_in, xT_out, w_proj, qscr, kres, vscr4, les):
    S, V, Td, G, GD, YD = mk.S, mk.V, mk.T, mk.G, mk.GD, mk.YD
    H, HD = 16, 64
    THETA = np.float32(100.00001)
    CP = np.float32(2e-3) / (THETA - np.float32(100.0))

    wp = mk.sb(mk.name("wp"), [128, NKC * 1024], BF16, les)
    for kc in range(NKC):
        GD.dma_start(out=wp[:, kc * 1024:(kc + 1) * 1024], in_=w_proj[kc * 128:(kc + 1) * 128, :])
    GD.inc()

    qbuf = [mk.sb(mk.name("qbuf"), [128, 512], BF16, les) for _ in range(2)]
    vbuf = [mk.sb(mk.name("vbuf"), [128, 32 * 64], BF16, les) for _ in range(2)]
    ubuf = mk.sb(mk.name("ubuf"), [128, 512], F32, les)
    hbuf = mk.sb(mk.name("hbuf"), [128, 512], F32, les)
    tbuf = mk.sb(mk.name("tbuf"), [128, 512], F32, les)
    rbuf = mk.sb(mk.name("rbuf"), [128, 512], F32, les)
    r2buf = mk.sb(mk.name("r2buf"), [128, 512], F32, les)
    abuf = [mk.sb(mk.name("abuf"), [128, 512], BF16, les) for _ in range(2)]
    sabuf = [mk.sb(mk.name("sabuf"), [128, 512], BF16, les) for _ in range(NKC)]
    xrbuf = mk.sb(mk.name("xrbuf"), [128, 512], F32, les)
    nxbuf = mk.sb(mk.name("nxbuf"), [128, 512], F32, les)
    thc = mk.ccol(float(THETA * CP))

    qtok = [0, 0]
    contok = [0, 0]
    su_tok = {}
    vr_tok = {}
    av_tok = {}
    gt_prev = [0]
    vr_prev = [0]
    va_prev = [0]
    evac_tok = {0: 0, 1: 0}

    GD.dma_start(out=qbuf[0][0:64, :], in_=qscr[0:64, 0:512])
    GD.dma_start(out=vbuf[0].rearrange("p (k f) -> p k f", f=64)[:, 0:4, :], in_=vscr4[0, :, 0:4, :])
    qtok[0] = GD.inc()

    for qt in range(NTT):
        nk = 4 * (qt + 1)
        for h in range(H):
            hb = (qt * H + h) % 2
            nhb = 1 - hb
            nh, nqt = (h + 1, qt) if h + 1 < H else (0, qt + 1)
            if nqt < NTT:
                nnk = 4 * (nqt + 1)
                GD.wait(Td, contok[nhb])
                GD.dma_start(out=qbuf[nhb][(nh % 2) * 64:(nh % 2) * 64 + 64, :], in_=qscr[nh * 64:(nh + 1) * 64, nqt * 512:(nqt + 1) * 512])
                GD.dma_start(out=vbuf[nhb].rearrange("p (k f) -> p k f", f=64)[:, 0:nnk, :],
                             in_=vscr4[nh, :, 0:nnk, :])
                qtok[nhb] = GD.inc()
            Yps = mk.pC if (h % 2 == 0) else mk.pD
            ktile = kres[h // 2]
            krow = (h % 2) * 64
            for kc in range(nk):
                par = kc % 2
                ps = mk.pA if par == 0 else mk.pB
                Td.wait(GD, qtok[hb])
                Td.wait(S, su_tok.get(par, 0))
                Td.wait(V, vr_tok.get(par, 0))
                Td.matmul(ps[:, :], ktile[krow:krow + 64, kc * 128:(kc + 1) * 128],
                          qbuf[hb][krow:krow + 64, :], start=True, stop=True)
                smt = Td.inc()
                S.wait(Td, smt)
                S.wait(G, gt_prev[0])
                S.wait(V, va_prev[0])
                S.activation(ubuf[:, :], ps[:, :], AF.Relu, scale=-1.0)
                su = S.inc()
                su_tok[par] = su
                S.ss()
                S.activation(hbuf[:, :], ubuf[:, :], AF.Relu, bias=thc[:, :], scale=float(-CP))
                sh2 = S.inc()
                G.wait(S, su)
                G.wait(V, va_prev[0])
                G.tensor_scalar(tbuf[:, :], ubuf[:, :], 1e-5, 1e-3, ALU.mult, ALU.add)
                gt = G.inc()
                gt_prev[0] = gt
                V.wait(S, su)
                V.scalar_tensor_tensor(rbuf[:, :], ps[:, :], 1.0, ubuf[:, :], ALU.mult, ALU.add)
                vr = V.inc()
                vr_tok[par] = vr
                vr_prev[0] = vr
                S.wait(V, vr)
                S.activation(r2buf[:, :], rbuf[:, :], AF.Square, scale=float(math.sqrt(1e-3)))
                s2 = S.inc()
                ab = abuf[par]
                V.wait(G, gt)
                V.wait(S, s2)
                V.wait(S, sh2)
                V.wait(Td, av_tok.get(par, 0))
                V.tensor_tensor(tbuf[:, :], tbuf[:, :], hbuf[:, :], ALU.min)
                V.ss()
                V.tensor_tensor(ab[:, :], r2buf[:, :], tbuf[:, :], ALU.add)
                off = kc * 128 - qt * 512
                if off >= 0:
                    if off > 0:
                        V.memset(ab[:, 0:off], 0.0)
                    V.tensor_tensor(ab[:, off:off + 128], ab[:, off:off + 128],
                                    mk.cmask[off][:, off:off + 128], ALU.mult)
                va = V.inc()
                va_prev[0] = va
                Td.wait(V, va)
                Td.wait(GD, qtok[hb])
                if kc == 0:
                    Td.wait(S, evac_tok[h % 2])
                Td.matmul(Yps[0:64, :], vbuf[hb][:, kc * 64:(kc + 1) * 64], ab[:, :],
                          start=(kc == 0), stop=(kc == nk - 1))
                av_tok[par] = Td.inc()
            contok[hb] = av_tok[(nk - 1) % 2]
            yt = contok[hb]
            S.wait(Td, yt)
            S.activation(sabuf[h // 2][(h % 2) * 64:(h % 2) * 64 + 64, :], Yps[0:64, :], AF.Copy)
            evac_tok[h % 2] = S.inc()
        sa_tok = S.inc()
        for m in range(NKC):
            ps = mk.pE if (m % 2 == 0) else mk.pF
            Td.wait(S, sa_tok)
            Td.wait(V)
            for kc in range(NKC):
                Td.matmul(ps[:, :], wp[:, kc * 1024 + m * 128: kc * 1024 + (m + 1) * 128],
                          sabuf[kc][:, :], start=(kc == 0), stop=(kc == NKC - 1))
            pt = Td.inc()
            GD.dma_start(out=xrbuf[:, :], in_=xT_in[m * 128:(m + 1) * 128, qt * 512:(qt + 1) * 512])
            xt_ = GD.inc()
            V.wait(Td, pt)
            V.wait(GD, xt_)
            V.scalar_tensor_tensor(nxbuf[:, :], ps[:, :], 1.0, xrbuf[:, :], ALU.mult, ALU.add)
            nt = V.inc()
            YD.wait(V, nt)
            YD.dma_start(out=xT_out[m * 128:(m + 1) * 128, qt * 512:(qt + 1) * 512], in_=nxbuf[:, :])
            yd = YD.inc()
            V.wait(YD, yd)
            GD.wait(V, nt)
    mk.barrier()


def compute_chead(mk, qscr_rows, kres, H, les, kv_heads=None, krow_of=None, kcols=T):
    S, V, Td, GD = mk.S, mk.V, mk.T, mk.GD
    nH = H if kv_heads is None else kv_heads
    sq = mk.sb(mk.name("csq"), [128, 512], F32, les)
    qld = mk.sb(mk.name("cqld"), [64, 512], BF16, les)
    mrow = mk.sb(mk.name("cmrow"), [1, 2 * H], F32, les)
    tmp1 = mk.sb(mk.name("ctmp1"), [1, 1], F32, les)
    biascols = mk.sb(mk.name("cbias"), [128, H], F32, les)
    for h in range(H):
        qs = qscr_rows(h)
        ncols = qs.shape[-1]
        for j in range(ncols // 512):
            GD.dma_start(out=qld[:, :], in_=qs[:, j * 512:(j + 1) * 512])
            dt = GD.inc()
            S.wait(GD, dt)
            S.wait(Td, Td.cnt)
            S.activation(sq[0:64, :], qld[:, :], AF.Square)
            st = S.inc()
            Td.wait(S, st)
            Td.matmul(mk.pG[0:1, :], mk.ones_col_f[0:64, :], sq[0:64, :], start=True, stop=True)
            mt = Td.inc()
            V.wait(Td, mt)
            if j == 0:
                V.tensor_reduce(mrow[0:1, h:h + 1], mk.pG[0:1, :], AX.X, ALU.max)
            else:
                V.tensor_reduce(tmp1[0:1, :], mk.pG[0:1, :], AX.X, ALU.max)
                V.ss()
                V.tensor_tensor(mrow[0:1, h:h + 1], mrow[0:1, h:h + 1], tmp1[0:1, :], ALU.max)
            vt = V.inc()
            Td.wait(V, vt)
            GD.wait(S, st)
    for h in range(nH):
        ktile, krow = krow_of(h)
        for j in range(kcols // 512):
            S.wait(Td, Td.cnt)
            S.activation(sq[krow:krow + 64, :], ktile[krow:krow + 64, j * 512:(j + 1) * 512], AF.Square)
            st = S.inc()
            Td.wait(S, st)
            Td.matmul(mk.pG[0:1, :], mk.ones_col_f[krow:krow + 64, :], sq[krow:krow + 64, :],
                      start=True, stop=True)
            mt = Td.inc()
            V.wait(Td, mt)
            if j == 0:
                V.tensor_reduce(mrow[0:1, H + h:H + h + 1], mk.pG[0:1, :], AX.X, ALU.max)
            else:
                V.tensor_reduce(tmp1[0:1, :], mk.pG[0:1, :], AX.X, ALU.max)
                V.ss()
                V.tensor_tensor(mrow[0:1, H + h:H + h + 1], mrow[0:1, H + h:H + h + 1], tmp1[0:1, :], ALU.max)
            vt = V.inc()
            Td.wait(V, vt)
    kv = (lambda h: h) if kv_heads is None else (lambda h: h // (H // kv_heads))
    negc = mk.sb(mk.name("cnegc"), [1, H], F32, les)
    vt = V.inc()
    V.ss()
    S.wait(V, vt)
    for h in range(H):
        V.tensor_tensor(negc[0:1, h:h + 1], mrow[0:1, h:h + 1], mrow[0:1, H + kv(h):H + kv(h) + 1], ALU.mult)
    vt = V.inc()
    V.ss()
    S.wait(V, vt)
    S.activation(negc[0:1, :], negc[0:1, :], AF.Sqrt)
    st = S.inc()
    V.wait(S, st)
    V.tensor_scalar(negc[0:1, :], negc[0:1, :], -1.02, None, ALU.mult)
    vt = V.inc()
    Td.wait(V, vt)
    for h in range(H):
        Td.wait(S, S.cnt)
        Td.matmul(mk.pG[:, 0:1], mk.ones_row_f[0:1, :], negc[0:1, h:h + 1], start=True, stop=True)
        mt = Td.inc()
        S.wait(Td, mt)
        S.activation(biascols[:, h:h + 1], mk.pG[:, 0:1], AF.Copy)
        S.inc()
    mk.barrier()
    return negc, biascols, mrow


def emit_softmax_attention(mk, cfg, les):
    S, V, Td, G, GD, YD = mk.S, mk.V, mk.T, mk.G, mk.GD, mk.YD
    H = cfg["H"]
    naug = cfg.get("naug", 0)
    causal = cfg["causal"]
    nq = 64 + naug
    qrow = cfg.get("qrow", lambda h: 0)
    qbuf = [mk.sb(mk.name("qbuf"), [128, 512], BF16, les) for _ in range(2)]
    vbuf = [mk.sb(mk.name("vbuf"), [128, 32 * 65], BF16, les) for _ in range(2)]
    abuf = [mk.sb(mk.name("abuf"), [128, 512], BF16, les) for _ in range(2)]
    yraw = mk.sb(mk.name("yraw"), [128, 512], F32, les)
    ynrm = [mk.sb(mk.name("ynrm"), [128, 512], F32, les) for _ in range(2)]
    srow = mk.sb(mk.name("srow"), [1, 512], F32, les)
    rrow = mk.sb(mk.name("rrow"), [1, 512], F32, les)
    for b in range(2):
        G.memset(vbuf[b].rearrange("p (k g) -> p k g", g=65)[:, :, 64:65], 1.0)
    G.inc()
    mk.barrier()

    qtok = [0, 0]
    contok = [0, 0]
    su_tok = {}
    av_tok = {}
    vmask_tok = {}
    evac_s = {0: 0, 1: 0}
    evac_v = {0: 0, 1: 0}
    wr_tok = {}
    nv_prev = [0]
    NQT = cfg.get("NQT", NTT)
    nk = cfg["nk"]

    def prefetch(h, qt, buf):
        GD.wait(Td, contok[buf])
        GD.dma_start(out=qbuf[buf][qrow(h):qrow(h) + nq, :], in_=cfg["q_slice"](h, qt))
        GD.dma_start(out=vbuf[buf].rearrange("p (k g) -> p k g", g=65)[:, 0:nk(qt), 0:64],
                     in_=cfg["vscr4"][cfg["kv"](h), :, 0:nk(qt), :])
        qtok[buf] = GD.inc()

    prefetch(0, 0, 0)
    for qt in range(NQT):
        nkq = nk(qt)
        for h in range(H):
            hb = (qt * H + h) % 2
            nhb = 1 - hb
            nh, nqt_ = (h + 1, qt) if h + 1 < H else (0, qt + 1)
            if nqt_ < NQT:
                prefetch(nh, nqt_, nhb)
            Yps = mk.pC if (h % 2 == 0) else mk.pD
            qr = qrow(h)
            for kc in range(nkq):
                par = kc % 2
                ps = mk.pA if par == 0 else mk.pB
                Td.wait(GD, qtok[hb])
                Td.wait(S, su_tok.get(par, 0))
                Td.matmul(ps[:, :], cfg["k_lhsT"](h, kc), qbuf[hb][qr:qr + nq, :], start=True, stop=True)
                smt = Td.inc()
                S.wait(Td, smt)
                S.wait(Td, av_tok.get(par, 0))
                S.wait(V, vmask_tok.get(par, 0))
                ab = abuf[par]
                S.activation(ab[:, :], ps[:, :], AF.Exp, bias=cfg["biascol"](h, kc))
                su = S.inc()
                su_tok[par] = su
                av_wait_v = 0
                if causal:
                    off = kc * 128 - qt * 512
                    if off >= 0:
                        V.wait(S, su)
                        if off > 0:
                            V.memset(ab[:, 0:off], 0.0)
                        V.tensor_tensor(ab[:, off:off + 128], ab[:, off:off + 128],
                                        mk.cmask[off][:, off:off + 128], ALU.mult)
                        va = V.inc()
                        vmask_tok[par] = va
                        av_wait_v = va
                Td.wait(S, su)
                if av_wait_v:
                    Td.wait(V, av_wait_v)
                if kc == 0:
                    Td.wait(S, evac_s[h % 2])
                    Td.wait(V, evac_v[h % 2])
                Td.matmul(Yps[0:65, :], vbuf[hb][:, kc * 65:kc * 65 + 65], ab[:, :],
                          start=(kc == 0), stop=(kc == nkq - 1))
                av_tok[par] = Td.inc()
            contok[hb] = av_tok[(nkq - 1) % 2]
            yt = contok[hb]
            V.wait(Td, yt)
            V.tensor_copy(srow[0:1, :], Yps[64:65, :])
            vt = V.inc()
            S.wait(V, vt)
            S.activation(rrow[0:1, :], srow[0:1, :], AF.Ln, scale=float(2.0 ** 48))
            S.ss()
            S.activation(rrow[0:1, :], rrow[0:1, :], AF.Exp, scale=-1.0,
                         bias=mk.ccol(48.0 * math.log(2.0))[0:1, :])
            rt = S.inc()
            Td.wait(S, rt)
            Td.matmul(mk.pG[0:64, :], mk.ones_row_f[0:1, 0:64], rrow[0:1, :], start=True, stop=True)
            bt = Td.inc()
            S.wait(Td, yt)
            S.wait(V, nv_prev[0])
            S.activation(yraw[0:64, :], Yps[0:64, :], AF.Copy)
            evac_s[h % 2] = S.inc()
            ypar = h % 2
            for ch, tokv in wr_tok.get(ypar, {}).items():
                V.wait(ch, tokv)
            V.wait(S, evac_s[h % 2])
            V.wait(Td, bt)
            ydst = cfg["ynrm_dest"](h, qt) if cfg.get("ynrm_dest") else ynrm[ypar][0:64, :]
            V.scalar_tensor_tensor(ydst, mk.pG[0:64, :], 1.0, yraw[0:64, :], ALU.mult, ALU.mult)
            nv = V.inc()
            nv_prev[0] = nv
            evac_v[h % 2] = nv
            Td.wait(V, nv)
            if cfg.get("sa_write"):
                wr_tok[ypar] = cfg["sa_write"](h, qt, ynrm[ypar], nv)
        cfg["after_qt"](qt)
    mk.barrier()


def emit_proj_residual(mk, xT_in, xT_out, sabuf, w_sb, bcols, les, extra_cb=None):
    S, V, Td, GD, YD = mk.S, mk.V, mk.T, mk.GD, mk.YD
    xrbuf = mk.sb(mk.name("xrbuf"), [128, 512], F32, les)
    nxbuf = mk.sb(mk.name("nxbuf"), [128, 512], F32, les)
    for qt in range(NTT):
        for m in range(NKC):
            ps = mk.pE if (m % 2 == 0) else mk.pF
            Td.wait(V)
            for kc in range(NKC):
                Td.matmul(ps[:, :], w_sb[:, kc * 1024 + m * 128: kc * 1024 + (m + 1) * 128],
                          sabuf[kc][:, qt * 512:(qt + 1) * 512], start=(kc == 0), stop=(kc == NKC - 1))
            pt = Td.inc()
            GD.dma_start(out=xrbuf[:, :], in_=xT_in[m * 128:(m + 1) * 128, qt * 512:(qt + 1) * 512])
            xt_ = GD.inc()
            V.wait(Td, pt)
            V.wait(GD, xt_)
            if bcols is not None:
                V.scalar_tensor_tensor(nxbuf[:, :], ps[:, :], bcols[:, m:m + 1], xrbuf[:, :], ALU.add, ALU.add)
            else:
                V.scalar_tensor_tensor(nxbuf[:, :], ps[:, :], 0.0, xrbuf[:, :], ALU.add, ALU.add)
            if extra_cb is not None:
                extra_cb(m, qt, nxbuf)
            nt = V.inc()
            YD.wait(V, nt)
            YD.dma_start(out=xT_out[m * 128:(m + 1) * 128, qt * 512:(qt + 1) * 512], in_=nxbuf[:, :])
            V.wait(YD, YD.cnt)
            GD.wait(V, nt)
    mk.barrier()


def load_bias_cols(mk, dst, b_ap, n, les=None, off=0):
    b2 = b_ap.rearrange("(m p) -> m p", p=128)
    for j in range(n):
        mk.GD.dma_start(out=dst[:, off + j:off + j + 1], in_=b2[j:j + 1, :].rearrange("m p -> p m"))
    return mk.GD.inc()


def emit_attn1(mk, xT_in, xT_out, w_qkv, b_qkv, w_out, b_out, qscr, vscr, kscr, outer_es):
    S, V, Td, G, GD, YD = mk.S, mk.V, mk.T, mk.G, mk.GD, mk.YD
    H = 16
    SCALE = (2.0 * 1024.0) ** -0.5
    qflat = qscr.rearrange("a b -> (a b)").rearrange("(r c) -> r c", r=64)
    vscr4 = vscr.rearrange("(h p k f) -> h p k f", h=16, p=128, k=32)

    with ExitStack() as les:
        murow = mk.sb(mk.name("murow"), [1, T], F32, les)
        rsrow = mk.sb(mk.name("rsrow"), [1, T], F32, les)
        xbuf = mk.sb(mk.name("xbuf"), [128, 1024], F32, les)
        sqbuf = mk.sb(mk.name("sqbuf"), [128, 1024], F32, les)
        tmprow = mk.sb(mk.name("tmprow"), [1, 512], F32, les)
        tmp512 = mk.sb(mk.name("tmp512"), [128, 512], F32, les)
        ln_stats(mk, xT_in, murow, rsrow, xbuf, sqbuf, tmprow)

        W = 2048
        lnbuf = mk.sb(mk.name("lnbuf"), [128, NKC * W], BF16, les)
        wq = mk.sb(mk.name("wq1"), [128, NKC * 3072], BF16, les)
        bq = mk.sb(mk.name("bq1"), [128, 24], F32, les)
        bqs = mk.sb(mk.name("bqs1"), [128, 24], F32, les)
        qs_stage = mk.sb(mk.name("qsst"), [64, 8192], BF16, les)
        vsub = mk.sb(mk.name("vsub"), [128, T], BF16, les)
        ksub = mk.sb(mk.name("ksub"), [128, T], BF16, les)
        vt_tile = mk.sb(mk.name("vt1"), [128, 128], BF16, les)
        for kc in range(NKC):
            GD.dma_start(out=wq[:, kc * 3072:(kc + 1) * 3072], in_=w_qkv[kc * 128:(kc + 1) * 128, :])
        load_bias_cols(mk, bq, b_qkv, 24)
        wt = GD.inc()
        V.wait(GD, wt)
        V.tensor_scalar(bqs[:, :], bq[:, :], float(SCALE), None, ALU.mult)
        V.inc()
        mk.barrier()

        qdma_tok = 0
        kdma_tok = 0
        for half in range(2):
            V.wait(Td)
            ptok = {}
            for i, tt in enumerate(range(half * 4, half * 4 + 4)):
                ln_apply_tile(mk, xT_in, murow, rsrow, tt, lnbuf, W, i * 512, xbuf, tmp512, ptok)
            ln_tok = V.inc()
            for t2 in range(4):
                tt = half * 4 + t2
                hA = 2 * tt
                for m in range(24):
                    ps = mk.pC if (m % 2 == 0) else mk.pD
                    Td.wait(V, ln_tok)
                    Td.wait(S)
                    Td.wait(V)
                    for kc in range(NKC):
                        Td.matmul(ps[:, :], wq[:, kc * 3072 + m * 128: kc * 3072 + (m + 1) * 128],
                                  lnbuf[:, kc * W + t2 * 512: kc * W + t2 * 512 + 512],
                                  start=(kc == 0), stop=(kc == NKC - 1))
                    mt = Td.inc()
                    S.wait(Td, mt)
                    for jj in range(2):
                        j = 2 * m + jj
                        m_sub, jm = j // 3, j % 3
                        src = ps[jj * 64:jj * 64 + 64, :]
                        bcol = bq[jj * 64:jj * 64 + 64, m:m + 1]
                        if jm == 0:
                            S.wait(YD, qdma_tok)
                            S.activation(qs_stage[:, :].rearrange("r (t x) -> r t x", x=16)[:, :, m_sub],
                                         src, AF.Identity, bias=bqs[jj * 64:jj * 64 + 64, m:m + 1],
                                         scale=float(SCALE))
                        elif jm == 1:
                            S.wait(YD, kdma_tok)
                            for hh in range(2):
                                S.activation(
                                    ksub[hh * 64:hh * 64 + 64, :].rearrange("p (s x) -> p s x", x=16)[:, :, m_sub],
                                    src[:, hh * 256:hh * 256 + 256], AF.Identity, bias=bcol)
                        else:
                            for hh in range(2):
                                S.activation(
                                    vsub[hh * 64:hh * 64 + 64, :].rearrange("p (s x) -> p s x", x=16)[:, :, m_sub],
                                    src[:, hh * 256:hh * 256 + 256], AF.Identity, bias=bcol)
                    S.inc()
                ev = S.inc()
                YD.wait(S, ev)
                YD.dma_start(out=qflat[:, tt * 8192:(tt + 1) * 8192], in_=qs_stage[:, :])
                qdma_tok = YD.inc()
                YD.dma_start(out=kscr[tt * 128:(tt + 1) * 128, :], in_=ksub[:, :])
                kdma_tok = YD.inc()
                for tc in range(32):
                    Td.wait(S, ev)
                    Td.wait(V)
                    Td.transpose(mk.pT[:, 0:128], vsub[:, tc * 128:(tc + 1) * 128], mk.ident[:])
                    ttr = Td.inc()
                    V.wait(Td, ttr)
                    V.tensor_copy(vt_tile[:, :], mk.pT[:, 0:128])
                    vtt = V.inc()
                    GD.wait(V, vtt)
                    GD.dma_start(out=vscr4[hA, :, tc, :], in_=vt_tile[:, 0:64])
                    GD.dma_start(out=vscr4[hA + 1, :, tc, :], in_=vt_tile[:, 64:128])
                    V.wait(GD, GD.cnt)
                S.wait(YD, qdma_tok)
                S.wait(Td)
        mk.barrier()
    kres = [mk.sb(mk.name("kres1"), [128, T], BF16, outer_es) for _ in range(8)]
    sabuf = [mk.sb(mk.name("sab1"), [128, T], BF16, outer_es) for _ in range(NKC)]
    for i in range(8):
        GD.dma_start(out=kres[i][:, :], in_=kscr[i * 128:(i + 1) * 128, :])
    GD.inc()
    mk.barrier()
    with ExitStack() as les:
        negc, biascols, _mrow = compute_chead(
            mk, lambda h: qflat[:, h * 4096:(h + 1) * 4096], kres, H, les,
            krow_of=lambda h: (kres[h // 2], (h % 2) * 64))

        def q_slice(h, qt):
            return qflat[0:64, h * 4096 + qt * 512: h * 4096 + (qt + 1) * 512]

        def sa_write(h, qt, ynrm, nv):
            for mp in range(16):
                V.tensor_copy(sabuf[mp // 2][(mp % 2) * 64:(mp % 2) * 64 + 64,
                                             256 * h + qt * 32: 256 * h + qt * 32 + 32],
                              ynrm[0:64, :].rearrange("p (j x) -> p j x", x=16)[:, :, mp])
            return {}

        cfg = dict(H=H, causal=True, naug=0,
                   q_slice=q_slice,
                   qrow=lambda h: (h % 2) * 64,
                   k_lhsT=lambda h, kc: kres[h // 2][(h % 2) * 64:(h % 2) * 64 + 64, kc * 128:(kc + 1) * 128],
                   biascol=lambda h, kc: biascols[:, h:h + 1],
                   vscr4=vscr4, kv=lambda h: h,
                   nk=lambda qt: 4 * (qt + 1),
                   sa_write=sa_write,
                   after_qt=lambda qt: None)
        emit_softmax_attention(mk, cfg, les)
    with ExitStack() as les:
        wo = mk.sb(mk.name("wo1"), [128, NKC * 1024], BF16, les)
        bo = mk.sb(mk.name("bo1"), [128, 8], F32, les)
        for kc in range(NKC):
            GD.dma_start(out=wo[:, kc * 1024:(kc + 1) * 1024], in_=w_out[kc * 128:(kc + 1) * 128, :])
        load_bias_cols(mk, bo, b_out, 8)
        GD.inc()
        mk.barrier()
        emit_proj_residual(mk, xT_in, xT_out, sabuf, wo, bo, les)


def emit_attn3(mk, xT_in, xT_out, w_q, b_q, w_k, b_k, w_v, b_v, w_o, b_o, slopes_ap,
               qscr, vscr, outer_es):
    S, V, Td, G, GD, YD = mk.S, mk.V, mk.T, mk.G, mk.GD, mk.YD
    H, KVH = 16, 4
    SCALE = 0.125
    vscr4 = vscr.rearrange("(h p k f) -> h p k f", h=16, p=128, k=32)
    kres3 = [mk.sb(mk.name("kres3"), [128, T], BF16, outer_es) for _ in range(KVH)]
    slopesb = mk.sb(mk.name("slopes3"), [1, 16], F32, outer_es)
    dbg_negc = mk.sb(mk.name("dbgnegc"), [1, 16], F32, outer_es)
    dbg_bias3 = mk.sb(mk.name("dbgbias3"), [128, 512], F32, outer_es)

    with ExitStack() as les:
        murow = mk.sb(mk.name("murow"), [1, T], F32, les)
        rsrow = mk.sb(mk.name("rsrow"), [1, T], F32, les)
        xbuf = mk.sb(mk.name("xbuf"), [128, 1024], F32, les)
        sqbuf = mk.sb(mk.name("sqbuf"), [128, 1024], F32, les)
        tmprow = mk.sb(mk.name("tmprow"), [1, 512], F32, les)
        tmp512 = mk.sb(mk.name("tmp512"), [128, 512], F32, les)
        ln_stats(mk, xT_in, murow, rsrow, xbuf, sqbuf, tmprow)

        W = 2048
        lnbuf = mk.sb(mk.name("lnbuf"), [128, NKC * W], BF16, les)
        wq = mk.sb(mk.name("wq3"), [128, NKC * 1024], BF16, les)
        wk = mk.sb(mk.name("wk3"), [128, NKC * 256], BF16, les)
        wv = mk.sb(mk.name("wv3"), [128, NKC * 256], BF16, les)
        bqc = mk.sb(mk.name("bqc3"), [128, 8], F32, les)
        bqs = mk.sb(mk.name("bqs3"), [128, 8], F32, les)
        bkc = mk.sb(mk.name("bkc3"), [128, 2], F32, les)
        bvc = mk.sb(mk.name("bvc3"), [128, 2], F32, les)
        qstage = mk.sb(mk.name("qst3"), [128, 512], BF16, les)
        GD.dma_start(out=slopesb[0:1, :], in_=slopes_ap.rearrange("(a s) -> a s", a=1))
        vstage = mk.sb(mk.name("vst3"), [128, 512], BF16, les)
        vt_tile = mk.sb(mk.name("vt3"), [128, 128], BF16, les)
        for kc in range(NKC):
            GD.dma_start(out=wq[:, kc * 1024:(kc + 1) * 1024], in_=w_q[kc * 128:(kc + 1) * 128, :])
            GD.dma_start(out=wk[:, kc * 256:(kc + 1) * 256], in_=w_k[kc * 128:(kc + 1) * 128, :])
            GD.dma_start(out=wv[:, kc * 256:(kc + 1) * 256], in_=w_v[kc * 128:(kc + 1) * 128, :])
        load_bias_cols(mk, bqc, b_q, 8)
        load_bias_cols(mk, bkc, b_k, 2)
        load_bias_cols(mk, bvc, b_v, 2)
        for i in range(KVH):
            G.memset(kres3[i][64:65, :], 1.0)
        wt = GD.inc()
        G.inc()
        V.wait(GD, wt)
        V.tensor_scalar(bqs[:, :], bqc[:, :], float(SCALE), None, ALU.mult)
        V.inc()
        mk.barrier()

        for half in range(2):
            V.wait(Td)
            ptok = {}
            for i, tt in enumerate(range(half * 4, half * 4 + 4)):
                ln_apply_tile(mk, xT_in, murow, rsrow, tt, lnbuf, W, i * 512, xbuf, tmp512, ptok)
            ln_tok = V.inc()
            for t2 in range(4):
                tt = half * 4 + t2
                tsl = slice(tt * 512, (tt + 1) * 512)
                for m in range(8):
                    ps = mk.pC if (m % 2 == 0) else mk.pD
                    Td.wait(V, ln_tok)
                    Td.wait(S)
                    for kc in range(NKC):
                        Td.matmul(ps[:, :], wq[:, kc * 1024 + m * 128: kc * 1024 + (m + 1) * 128],
                                  lnbuf[:, kc * W + t2 * 512: kc * W + t2 * 512 + 512],
                                  start=(kc == 0), stop=(kc == NKC - 1))
                    mt = Td.inc()
                    S.wait(Td, mt)
                    S.wait(YD, YD.cnt)
                    S.activation(qstage[:, :], ps[:, :], AF.Identity, bias=bqs[:, m:m + 1], scale=SCALE)
                    ev = S.inc()
                    YD.wait(S, ev)
                    for jj in range(2):
                        h = 2 * m + jj
                        YD.dma_start(out=qscr[h * 65:h * 65 + 64, tsl], in_=qstage[jj * 64:jj * 64 + 64, :])
                    YD.inc()
                for part, wb, bc, dst_is_k in ((0, wk, bkc, True), (1, wv, bvc, False)):
                    for m in range(2):
                        ps = mk.pC if (m % 2 == 0) else mk.pD
                        Td.wait(V, ln_tok)
                        Td.wait(S)
                        Td.wait(V)
                        for kc in range(NKC):
                            Td.matmul(ps[:, :], wb[:, kc * 256 + m * 128: kc * 256 + (m + 1) * 128],
                                      lnbuf[:, kc * W + t2 * 512: kc * W + t2 * 512 + 512],
                                      start=(kc == 0), stop=(kc == NKC - 1))
                        mt = Td.inc()
                        S.wait(Td, mt)
                        if dst_is_k:
                            for jj in range(2):
                                kvh = 2 * m + jj
                                S.activation(kres3[kvh][0:64, tsl], ps[jj * 64:jj * 64 + 64, :],
                                             AF.Identity, bias=bc[jj * 64:jj * 64 + 64, m:m + 1])
                            S.inc()
                        else:
                            S.wait(Td)
                            S.activation(vstage[:, :], ps[:, :], AF.Identity, bias=bc[:, m:m + 1])
                            ev = S.inc()
                            for tc in range(4):
                                kchunk = tt * 4 + tc
                                Td.wait(S, ev)
                                Td.wait(V)
                                Td.transpose(mk.pT[:, 0:128], vstage[:, tc * 128:(tc + 1) * 128], mk.ident[:])
                                ttr = Td.inc()
                                V.wait(Td, ttr)
                                V.tensor_copy(vt_tile[:, :], mk.pT[:, 0:128])
                                vtt = V.inc()
                                GD.wait(V, vtt)
                                GD.dma_start(out=vscr4[2 * m, :, kchunk, :], in_=vt_tile[:, 0:64])
                                GD.dma_start(out=vscr4[2 * m + 1, :, kchunk, :], in_=vt_tile[:, 64:128])
                                V.wait(GD, GD.cnt)
        mk.barrier()

    sabuf = [mk.sb(mk.name("sab3"), [128, T], BF16, outer_es) for _ in range(NKC)]
    with ExitStack() as les:
        negc, biascols, mrow = compute_chead(
            mk, lambda h: qscr[h * 65:h * 65 + 64, :], kres3, H, les,
            kv_heads=KVH, krow_of=lambda kvh: (kres3[kvh], 0))
        bias3 = mk.sb(mk.name("bias3"), [128, 512], F32, les)
        iota_r = mk.sb(mk.name("iota_r"), [1, 128], F32, les)
        srow = mk.sb(mk.name("srow3"), [1, 512], F32, les)
        kcrow = mk.sb(mk.name("kcrow"), [1, 512], F32, les)
        drow = mk.sb(mk.name("drow"), [1, 512], F32, les)
        iota_t = mk.sb(mk.name("iota_t"), [1, T], F32, les)
        qld3 = mk.sb(mk.name("qld3"), [64, 512], BF16, les)
        prod3 = mk.sb(mk.name("prod3"), [64, 512], F32, les)
        sdrow = mk.sb(mk.name("sdrow"), [1, 512], F32, les)
        augbf = mk.sb(mk.name("augbf"), [1, 512], BF16, les)
        G.iota(iota_r[:, :], pattern=[[1, 128]], base=0, channel_multiplier=0,
               allow_small_or_imprecise_dtypes=True)
        G.iota(kcrow[:, :], pattern=[[0, 16], [128, 32]], base=0, channel_multiplier=0,
               allow_small_or_imprecise_dtypes=True)
        G.iota(iota_t[:, :], pattern=[[1, T]], base=0, channel_multiplier=0,
               allow_small_or_imprecise_dtypes=True)
        gt = G.inc()
        V.wait(G, gt)
        t1r = mk.sb(mk.name("t1r3"), [1, 16], F32, les)
        t2r = mk.sb(mk.name("t2r3"), [1, 16], F32, les)
        Mrow = mk.sb(mk.name("Mrow3"), [1, 16], F32, les)
        S.wait(V, V.cnt)
        S.activation(t1r[0:1, :], slopesb[0:1, :], AF.Relu, scale=-4095.0)
        S.activation(t2r[0:1, :], slopesb[0:1, :], AF.Relu, scale=127.0)
        st = S.inc()
        V.wait(S, st)
        V.tensor_scalar(Mrow[0:1, :], negc[0:1, :], -2.0, -80.0, ALU.mult, ALU.add)
        V.ss()
        V.tensor_tensor(Mrow[0:1, :], Mrow[0:1, :], t1r[0:1, :], ALU.add)
        V.ss()
        V.tensor_tensor(Mrow[0:1, :], Mrow[0:1, :], t2r[0:1, :], ALU.add)
        vt0 = V.inc()
        S.wait(V, vt0)
        S.activation(Mrow[0:1, :], Mrow[0:1, :], AF.Relu)
        st = S.inc()
        V.wait(S, st)
        V.tensor_scalar(Mrow[0:1, :], Mrow[0:1, :], -1.0, None, ALU.mult)
        V.ss()
        for h in range(H):
            V.tensor_scalar(srow[0:1, h * 32:(h + 1) * 32], mk.ones_row_f[0:1, 0:32],
                            slopesb[0:1, h:h + 1], None, ALU.mult)
            V.tensor_scalar(drow[0:1, h * 32:(h + 1) * 32], kcrow[0:1, h * 32:(h + 1) * 32],
                            slopesb[0:1, h:h + 1], None, ALU.mult)
            V.ss()
            V.tensor_scalar(drow[0:1, h * 32:(h + 1) * 32], drow[0:1, h * 32:(h + 1) * 32],
                            Mrow[0:1, h:h + 1], None, ALU.add)
            V.ss()
        vt = V.inc()
        for h in range(H):
            kvh = h // 4
            for tt in range(NTT):
                tsl3 = slice(tt * 512, (tt + 1) * 512)
                GD.wait(V)
                GD.dma_start(out=qld3[:, :], in_=qscr[h * 65:h * 65 + 64, tsl3])
                qdt = GD.inc()
                V.wait(GD, qdt)
                V.tensor_tensor(prod3[:, :], qld3[:, :], kres3[kvh][0:64, tsl3], ALU.mult)
                pv = V.inc()
                Td.wait(V, pv)
                Td.wait(S)
                Td.matmul(mk.pG[0:1, :], mk.ones_col_f[0:64, :], prod3[:, :], start=True, stop=True)
                mt3 = Td.inc()
                S.wait(Td, mt3)
                S.activation(sdrow[0:1, :], mk.pG[0:1, :], AF.Copy)
                sd3 = S.inc()
                V.wait(S, sd3)
                V.wait(YD, YD.cnt)
                V.tensor_scalar(augbf[0:1, :], iota_t[0:1, tsl3], slopesb[0:1, h:h + 1], -1.0,
                                ALU.mult, ALU.mult)
                V.ss()
                V.tensor_tensor(augbf[0:1, :], augbf[0:1, :], sdrow[0:1, :], ALU.subtract)
                av3 = V.inc()
                YD.wait(V, av3)
                YD.dma_start(out=qscr[h * 65 + 64:h * 65 + 65, tsl3], in_=augbf[0:1, :])
        YD.inc()
        Td.wait(V, vt)
        Td.wait(V)
        Td.wait(S)
        Td.matmul(mk.pG[:, :], iota_r[0:1, :], srow[0:1, :], start=True, stop=False)
        Td.matmul(mk.pG[:, :], mk.ones_row_f[0:1, :], drow[0:1, :], start=False, stop=True)
        bt = Td.inc()
        S.wait(Td, bt)
        S.activation(bias3[:, :], mk.pG[:, :], AF.Copy)
        S.inc()
        V.wait(S, S.cnt)
        V.tensor_copy(dbg_negc[0:1, :], negc[0:1, :])
        V.tensor_copy(dbg_bias3[:, :], bias3[:, :])
        V.tensor_copy(dbg_bias3[0:1, 480:512], mrow[0:1, 0:32])
        V.inc()
        mk.barrier()

        cfg = dict(H=H, causal=True, naug=1,
                   q_slice=lambda h, qt: qscr[h * 65:h * 65 + 65, qt * 512:(qt + 1) * 512],
                   qrow=lambda h: 0,
                   k_lhsT=lambda h, kc: kres3[h // 4][0:65, kc * 128:(kc + 1) * 128],
                   biascol=lambda h, kc: bias3[:, h * 32 + kc:h * 32 + kc + 1],
                   vscr4=vscr4, kv=lambda h: h // 4,
                   nk=lambda qt: 4 * (qt + 1),
                   ynrm_dest=lambda h, qt: sabuf[h // 2][(h % 2) * 64:(h % 2) * 64 + 64,
                                                         qt * 512:(qt + 1) * 512],
                   sa_write=None,
                   after_qt=lambda qt: None)
        emit_softmax_attention(mk, cfg, les)

    with ExitStack() as les:
        wo = mk.sb(mk.name("wo3"), [128, NKC * 1024], BF16, les)
        bo = mk.sb(mk.name("bo3"), [128, 8], F32, les)
        for kc in range(NKC):
            GD.dma_start(out=wo[:, kc * 1024:(kc + 1) * 1024], in_=w_o[kc * 128:(kc + 1) * 128, :])
        load_bias_cols(mk, bo, b_o, 8)
        GD.inc()
        mk.barrier()
        emit_proj_residual(mk, xT_in, xT_out, sabuf, wo, bo, les)
    return dict(sabuf=sabuf, kres3=kres3, dbg_negc=dbg_negc, dbg_bias3=dbg_bias3)


def emit_attn2(mk, xT_in, xT_out, w_q, b_q, w_k, b_k, w_v, b_v, w_o, b_o,
               qscr, vscr, outer_es):
    S, V, Td, G, GD, YD = mk.S, mk.V, mk.T, mk.G, mk.GD, mk.YD
    H = 16
    vscr4 = vscr.rearrange("(h p k f) -> h p k f", h=16, p=128, k=32)
    kres = [mk.sb(mk.name("kres2"), [128, T], BF16, outer_es) for _ in range(8)]

    with ExitStack() as les:
        murow = mk.sb(mk.name("murow"), [1, T], F32, les)
        rsrow = mk.sb(mk.name("rsrow"), [1, T], F32, les)
        xbuf = mk.sb(mk.name("xbuf"), [128, 1024], F32, les)
        sqbuf = mk.sb(mk.name("sqbuf"), [128, 1024], F32, les)
        tmprow = mk.sb(mk.name("tmprow"), [1, 512], F32, les)
        tmp512 = mk.sb(mk.name("tmp512"), [128, 512], F32, les)
        ln_stats(mk, xT_in, murow, rsrow, xbuf, sqbuf, tmprow)

        W = 2048
        lnbuf = mk.sb(mk.name("lnbuf"), [128, NKC * W], BF16, les)
        wstage = mk.sb(mk.name("wstg2"), [128, 1024], BF16, les)
        wq2 = mk.sb(mk.name("wq2"), [128, NKC * 1024], BF16, les)
        wk2 = mk.sb(mk.name("wk2"), [128, NKC * 1024], BF16, les)
        wv2 = mk.sb(mk.name("wv2"), [128, NKC * 1024], BF16, les)
        bqc = mk.sb(mk.name("bqc2"), [128, 8], F32, les)
        bkc = mk.sb(mk.name("bkc2"), [128, 8], F32, les)
        bvc = mk.sb(mk.name("bvc2"), [128, 8], F32, les)
        qstage = mk.sb(mk.name("qst2"), [128, 512], BF16, les)
        vstage = mk.sb(mk.name("vst2"), [128, 512], BF16, les)
        vt_tile = mk.sb(mk.name("vt2"), [128, 128], BF16, les)
        load_weight_T(mk, wq2, w_q, 1024, 1024, wstage)
        load_weight_T(mk, wk2, w_k, 1024, 1024, wstage)
        load_weight_T(mk, wv2, w_v, 1024, 1024, wstage)
        load_bias_cols(mk, bqc, b_q, 8)
        load_bias_cols(mk, bkc, b_k, 8)
        load_bias_cols(mk, bvc, b_v, 8)
        mk.barrier()

        for half in range(2):
            V.wait(Td)
            ptok = {}
            for i, tt in enumerate(range(half * 4, half * 4 + 4)):
                ln_apply_tile(mk, xT_in, murow, rsrow, tt, lnbuf, W, i * 512, xbuf, tmp512, ptok)
            ln_tok = V.inc()
            for t2 in range(4):
                tt = half * 4 + t2
                tsl = slice(tt * 512, (tt + 1) * 512)
                for part, wb, bc in ((0, wq2, bqc), (1, wk2, bkc), (2, wv2, bvc)):
                    for m in range(8):
                        ps = mk.pC if (m % 2 == 0) else mk.pD
                        Td.wait(V, ln_tok)
                        Td.wait(S)
                        Td.wait(V)
                        for kc in range(NKC):
                            Td.matmul(ps[:, :], wb[:, kc * 1024 + m * 128: kc * 1024 + (m + 1) * 128],
                                      lnbuf[:, kc * W + t2 * 512: kc * W + t2 * 512 + 512],
                                      start=(kc == 0), stop=(kc == NKC - 1))
                        mt = Td.inc()
                        S.wait(Td, mt)
                        if part == 0:
                            S.wait(YD, YD.cnt)
                            S.activation(qstage[:, :], ps[:, :], AF.Identity, bias=bqc[:, m:m + 1])
                            ev = S.inc()
                            YD.wait(S, ev)
                            YD.dma_start(out=qscr[m * 128:(m + 1) * 128, tsl], in_=qstage[:, :])
                            YD.inc()
                        elif part == 1:
                            S.activation(kres[m][:, tsl], ps[:, :], AF.Identity, bias=bkc[:, m:m + 1])
                            S.inc()
                        else:
                            S.wait(Td)
                            S.activation(vstage[:, :], ps[:, :], AF.Identity, bias=bvc[:, m:m + 1])
                            ev = S.inc()
                            for tc in range(4):
                                kchunk = tt * 4 + tc
                                Td.wait(S, ev)
                                Td.wait(V)
                                Td.transpose(mk.pT[:, 0:128], vstage[:, tc * 128:(tc + 1) * 128], mk.ident[:])
                                ttr = Td.inc()
                                V.wait(Td, ttr)
                                V.tensor_copy(vt_tile[:, :], mk.pT[:, 0:128])
                                vtt = V.inc()
                                GD.wait(V, vtt)
                                GD.dma_start(out=vscr4[2 * m, :, kchunk, :], in_=vt_tile[:, 0:64])
                                GD.dma_start(out=vscr4[2 * m + 1, :, kchunk, :], in_=vt_tile[:, 64:128])
                                V.wait(GD, GD.cnt)
        mk.barrier()

    sabuf = [mk.sb(mk.name("sab2"), [128, T], BF16, outer_es) for _ in range(NKC)]
    with ExitStack() as les:
        qbuf = [mk.sb(mk.name("qbuf2"), [128, 512], BF16, les) for _ in range(2)]
        vbuf = [mk.sb(mk.name("vbuf2"), [128, 32 * 64], BF16, les) for _ in range(2)]
        abuf = [mk.sb(mk.name("abuf2"), [128, 512], BF16, les) for _ in range(2)]
        qtok = [0, 0]
        contok = [0, 0]
        ev_tok = {}
        av_tok = {}
        vm_tok = {}
        evac_tok = {0: 0, 1: 0}

        def prefetch(h, qt, buf):
            GD.wait(Td, contok[buf])
            GD.dma_start(out=qbuf[buf][(h % 2) * 64:(h % 2) * 64 + 64, :],
                         in_=qscr[h * 64:(h + 1) * 64, qt * 512:(qt + 1) * 512])
            GD.dma_start(out=vbuf[buf].rearrange("p (k f) -> p k f", f=64)[:, 0:4 * (qt + 1), :],
                         in_=vscr4[h, :, 0:4 * (qt + 1), :])
            qtok[buf] = GD.inc()

        prefetch(0, 0, 0)
        for qt in range(NTT):
            nk = 4 * (qt + 1)
            for h in range(16):
                hb = (qt * 16 + h) % 2
                nhb = 1 - hb
                nh, nqt = (h + 1, qt) if h + 1 < 16 else (0, qt + 1)
                if nqt < NTT:
                    prefetch(nh, nqt, nhb)
                Yps = mk.pC if (h % 2 == 0) else mk.pD
                ktile = kres[h // 2]
                krow = (h % 2) * 64
                for kc in range(nk):
                    par = kc % 2
                    ps = mk.pA if par == 0 else mk.pB
                    Td.wait(GD, qtok[hb])
                    Td.wait(S, ev_tok.get(par, 0))
                    Td.matmul(ps[:, :], ktile[krow:krow + 64, kc * 128:(kc + 1) * 128],
                              qbuf[hb][krow:krow + 64, :], start=True, stop=True)
                    smt = Td.inc()
                    S.wait(Td, smt)
                    S.wait(Td, av_tok.get(par, 0))
                    S.wait(V, vm_tok.get(par, 0))
                    ab = abuf[par]
                    S.activation(ab[:, :], ps[:, :], AF.Copy)
                    su = S.inc()
                    ev_tok[par] = su
                    av_wait_v = 0
                    off = kc * 128 - qt * 512
                    if off >= 0:
                        V.wait(S, su)
                        if off > 0:
                            V.memset(ab[:, 0:off], 0.0)
                        V.tensor_tensor(ab[:, off:off + 128], ab[:, off:off + 128],
                                        mk.cmask[off][:, off:off + 128], ALU.mult)
                        va = V.inc()
                        vm_tok[par] = va
                        av_wait_v = va
                    Td.wait(S, su)
                    if av_wait_v:
                        Td.wait(V, av_wait_v)
                    if kc == 0:
                        Td.wait(S, evac_tok[h % 2])
                    Td.matmul(Yps[0:64, :], vbuf[hb][:, kc * 64:(kc + 1) * 64], ab[:, :],
                              start=(kc == 0), stop=(kc == nk - 1))
                    av_tok[par] = Td.inc()
                contok[hb] = av_tok[(nk - 1) % 2]
                yt = contok[hb]
                S.wait(Td, yt)
                S.activation(sabuf[h // 2][(h % 2) * 64:(h % 2) * 64 + 64, qt * 512:(qt + 1) * 512],
                             Yps[0:64, :], AF.Copy)
                evac_tok[h % 2] = S.inc()
        mk.barrier()

    with ExitStack() as les:
        wo2 = mk.sb(mk.name("wo2"), [128, NKC * 1024], BF16, les)
        wstage = mk.sb(mk.name("wstg2b"), [128, 1024], BF16, les)
        bo = mk.sb(mk.name("bo2"), [128, 8], F32, les)
        load_weight_T(mk, wo2, w_o, 1024, 1024, wstage)
        load_bias_cols(mk, bo, b_o, 8)
        GD.inc()
        mk.barrier()
        emit_proj_residual(mk, xT_in, xT_out, sabuf, wo2, bo, les)


def emit_ffn0(mk, xT_in, xT_out, w_in, b_in, w1, b1, w2, b2, w_out, b_out, outer_es):
    S, V, Td, G, GD, YD = mk.S, mk.V, mk.T, mk.G, mk.GD, mk.YD
    with ExitStack() as les:
        murow = mk.sb(mk.name("murow"), [1, T], F32, les)
        rsrow = mk.sb(mk.name("rsrow"), [1, T], F32, les)
        xbuf = mk.sb(mk.name("xbuf"), [128, 1024], F32, les)
        sqbuf = mk.sb(mk.name("sqbuf"), [128, 1024], F32, les)
        tmprow = mk.sb(mk.name("tmprow"), [1, 512], F32, les)
        tmp512 = mk.sb(mk.name("tmp512"), [128, 512], F32, les)
        ln_stats(mk, xT_in, murow, rsrow, xbuf, sqbuf, tmprow)

        W = T
        lnbuf = mk.sb(mk.name("lnbuf"), [128, NKC * W], BF16, les)
        win = mk.sb(mk.name("win0"), [128, NKC * 64], BF16, les)
        wout = mk.sb(mk.name("wout0"), [64, 1024], BF16, les)
        w1sb = mk.sb(mk.name("w1sb"), [64, 16], F32, les)
        b1sb = mk.sb(mk.name("b1sb"), [64, 16], F32, les)
        w2sb = mk.sb(mk.name("w2sb"), [64, 16], F32, les)
        binc = mk.sb(mk.name("binc"), [64, 1], F32, les)
        b2c = mk.sb(mk.name("b2c"), [64, 1], F32, les)
        boc = mk.sb(mk.name("boc0"), [128, 8], F32, les)
        s_sb = mk.sb(mk.name("s_sb"), [64, 512], F32, les)
        tb = mk.sb(mk.name("tb0"), [64, 512], F32, les)
        acc = [mk.sb(mk.name("acc0"), [64, 512], F32, les) for _ in range(2)]
        gbf = mk.sb(mk.name("gbf"), [64, 512], BF16, les)
        xrbuf = mk.sb(mk.name("xr0"), [128, 512], F32, les)
        nxbuf = mk.sb(mk.name("nx0"), [128, 512], F32, les)
        for kc in range(NKC):
            GD.dma_start(out=win[:, kc * 64:(kc + 1) * 64], in_=w_in[kc * 128:(kc + 1) * 128, :])
        GD.dma_start(out=wout[0:64, :], in_=w_out[:, :])
        GD.dma_start(out=w1sb[0:64, :], in_=w1[:, :])
        GD.dma_start(out=b1sb[0:64, :], in_=b1[:, :])
        GD.dma_start(out=w2sb[0:64, :], in_=w2[:, :])
        GD.dma_start(out=binc[0:64, :], in_=b_in.rearrange("(a b) -> a b", b=1))
        GD.dma_start(out=b2c[0:64, :], in_=b2.rearrange("(a b) -> a b", b=1))
        load_bias_cols(mk, boc, b_out, 8)
        GD.inc()
        mk.barrier()

        ptok = {}
        for tt in range(NTT):
            ln_apply_tile(mk, xT_in, murow, rsrow, tt, lnbuf, W, tt * 512, xbuf, tmp512, ptok)
        ln_tok = V.inc()
        for tt in range(NTT):
            tsl = slice(tt * 512, (tt + 1) * 512)
            Td.wait(V, ln_tok)
            Td.wait(S)
            for kc in range(NKC):
                Td.matmul(mk.pC[0:64, :], win[:, kc * 64:(kc + 1) * 64],
                          lnbuf[:, kc * W + tt * 512: kc * W + tt * 512 + 512],
                          start=(kc == 0), stop=(kc == NKC - 1))
            mt = Td.inc()
            S.wait(Td, mt)
            S.wait(V)
            S.activation(s_sb[:, :], mk.pC[0:64, :], AF.Identity, bias=binc[0:64, :])
            st = S.inc()
            V.wait(S, st)
            for i in range(16):
                V.tensor_scalar(tb[:, :], s_sb[:, :], w1sb[0:64, i:i + 1], b1sb[0:64, i:i + 1],
                                ALU.mult, ALU.add)
                V.ss()
                V.tensor_scalar(tb[:, :], tb[:, :], 0.0, None, ALU.max)
                V.ss()
                if i == 0:
                    V.tensor_scalar(acc[0][:, :], tb[:, :], w2sb[0:64, i:i + 1], None, ALU.mult)
                else:
                    V.scalar_tensor_tensor(acc[i % 2][:, :], tb[:, :], w2sb[0:64, i:i + 1],
                                           acc[(i - 1) % 2][:, :], ALU.mult, ALU.add)
            V.tensor_scalar(gbf[:, :], acc[15 % 2][:, :], b2c[0:64, :], None, ALU.add)
            gt = V.inc()
            for m in range(NKC):
                ps = mk.pE if (m % 2 == 0) else mk.pF
                Td.wait(V)
                Td.matmul(ps[:, :], wout[0:64, m * 128:(m + 1) * 128], gbf[:, :],
                          start=True, stop=True)
                pt = Td.inc()
                GD.dma_start(out=xrbuf[:, :], in_=xT_in[m * 128:(m + 1) * 128, tsl])
                xt_ = GD.inc()
                V.wait(Td, pt)
                V.wait(GD, xt_)
                V.scalar_tensor_tensor(nxbuf[:, :], ps[:, :], boc[:, m:m + 1], xrbuf[:, :],
                                       ALU.add, ALU.add)
                nt = V.inc()
                YD.wait(V, nt)
                YD.dma_start(out=xT_out[m * 128:(m + 1) * 128, tsl], in_=nxbuf[:, :])
                V.wait(YD, YD.cnt)
                GD.wait(V, nt)
    mk.barrier()


def emit_ffn3(mk, xT_in, xT_out, wi, bi, wo, bo, lora_A, lora_B, gate_w, gate_b, sscr, mskscr, valscr, outer_es, dbg=None):
    S, V, Td, G, GD, YD = mk.S, mk.V, mk.T, mk.G, mk.GD, mk.YD
    E, R, DFF = 8, 16, 4096
    with ExitStack() as les:
        xbuf = mk.sb(mk.name("xbuf"), [128, 1024], F32, les)
        tmp512 = mk.sb(mk.name("tmp512"), [128, 512], F32, les)
        with ExitStack() as ses:
            murow = mk.sb(mk.name("murow"), [1, T], F32, ses)
            rsrow = mk.sb(mk.name("rsrow"), [1, T], F32, ses)
            sqbuf = mk.sb(mk.name("sqbuf"), [128, 1024], F32, ses)
            tmprow = mk.sb(mk.name("tmprow"), [1, 512], F32, ses)
            ln_stats(mk, xT_in, murow, rsrow, xbuf, sqbuf, tmprow)
            YD.wait(mk.S)
            YD.wait(mk.V)
            YD.dma_start(out=sscr[0:1, :], in_=murow[0:1, :])
            YD.dma_start(out=sscr[1:2, :], in_=rsrow[0:1, :])
            mk.barrier()

        wstage = mk.sb(mk.name("wstg3"), [128, 1024], BF16, les)
        wiT = mk.sb(mk.name("wiT"), [128, NKC * DFF], BF16, les)
        AT = mk.sb(mk.name("AT3"), [128, NKC * 128], BF16, les)
        Bst = [mk.sb(mk.name("Bst"), [64, 32 * 128], BF16, les) for _ in range(2)]
        gwT = mk.sb(mk.name("gwT"), [128, NKC * 8], BF16, les)
        bic = mk.sb(mk.name("bic"), [128, 32], F32, les)
        gbc = mk.sb(mk.name("gbc"), [8, 1], F32, les)
        bo_row = mk.sb(mk.name("bo_row"), [1, 1024], BF16, les)
        twos = mk.sb(mk.name("twos"), [1, 512], BF16, les)
        m1T = mk.sb(mk.name("m1T"), [8, 512], BF16, les)
        m2T = mk.sb(mk.name("m2T"), [8, 512], BF16, les)
        valT = mk.sb(mk.name("valT"), [1, 512], F32, les)
        murow = mk.sb(mk.name("mu3"), [1, 512], F32, les)
        rsrow = mk.sb(mk.name("rs3"), [1, 512], F32, les)
        E8 = [mk.sb(mk.name("E8"), [8, 64], BF16, les) for _ in range(2)]

        load_weight_T(mk, wiT, wi, DFF, D, wstage)
        load_weight_T(mk, AT, lora_A.rearrange("e r d -> (e r) d"), 128, D, wstage)
        load_weight_T(mk, gwT, gate_w, 8, D, wstage)
        for i in range(2):
            for ep in range(2):
                e0 = i * 4 + ep * 2
                for mf in range(32):
                    GD.wait(Td)
                    GD.dma_start(out=wstage[:, 0:16], in_=lora_B[e0, mf * 128:(mf + 1) * 128, :])
                    GD.dma_start(out=wstage[:, 16:32], in_=lora_B[e0 + 1, mf * 128:(mf + 1) * 128, :])
                    t = GD.inc()
                    Td.wait(GD, t)
                    Td.wait(V)
                    Td.transpose(mk.pT[0:32, 0:128], wstage[0:128, 0:32], mk.ident[:])
                    tt_ = Td.inc()
                    V.wait(Td, tt_)
                    V.tensor_copy(Bst[i][ep * 32:ep * 32 + 32, mf * 128:(mf + 1) * 128],
                                  mk.pT[0:32, 0:128])
                    V.inc()
        load_bias_cols(mk, bic, bi, 32)
        GD.dma_start(out=gbc[0:8, :], in_=gate_b.rearrange("(a b) -> a b", b=1))
        GD.dma_start(out=bo_row[0:1, :], in_=bo.rearrange("(a b) -> a b", a=1))
        G.memset(twos[:], 2.0)
        ramp8 = mk.sb(mk.name("ramp8"), [8, 64], F32, les)
        G.iota(ramp8[:, :], pattern=[[1, 4], [0, 16]], base=0, channel_multiplier=0,
               allow_small_or_imprecise_dtypes=True)
        gt = G.inc()
        V.wait(G, gt)
        for i in range(2):
            V.tensor_scalar(E8[i][:, :], ramp8[:, :], float(4 * i), mk.iota_col[0:8, :],
                            ALU.add, ALU.is_equal)
        V.inc()
        mk.barrier()

        W = 512
        lnt = mk.sb(mk.name("lnt3"), [128, NKC * W], BF16, les)
        gstage = mk.sb(mk.name("gst"), [8, 512], F32, les)
        gtok = mk.sb(mk.name("gtok"), [128, 8], F32, les)
        gsc = mk.sb(mk.name("gsc"), [128, 8], F32, les)
        mcol = mk.sb(mk.name("mcol"), [128, 1], F32, les)
        vcol = mk.sb(mk.name("vcol"), [128, 1], F32, les)
        rcol = mk.sb(mk.name("rcol"), [128, 1], F32, les)
        tmpf = mk.sb(mk.name("tmpf"), [128, 1], F32, les)
        s8 = mk.sb(mk.name("s8"), [128, 8], F32, les)
        msk = mk.sb(mk.name("msk"), [128, 17], F32, les)
        la_sb = mk.sb(mk.name("la_sb"), [128, 512], BF16, les)
        mla1 = [mk.sb(mk.name("mla1"), [64, 512], BF16, les) for _ in range(2)]
        mla2 = [mk.sb(mk.name("mla2"), [64, 512], BF16, les) for _ in range(2)]
        bsb = mk.sb(mk.name("bsb3"), [128, 512], F32, les)
        r1 = mk.sb(mk.name("r1"), [128, 512], F32, les)
        r2 = mk.sb(mk.name("r2"), [128, 512], F32, les)
        hbuf = mk.sb(mk.name("hbuf3"), [128, 32 * 512], BF16, les)
        woT = [mk.sb(mk.name("woT"), [128, 33 * 128], BF16, les) for _ in range(2)]
        psb = mk.sb(mk.name("psb3"), [128, 512], F32, les)
        xrbuf = mk.sb(mk.name("xr3"), [128, 512], F32, les)
        nxbuf = mk.sb(mk.name("nx3"), [128, 512], F32, les)

        lptok = {}

        def ln_tt(tt):
            V.wait(Td)
            GD.wait(V)
            GD.dma_start(out=murow[0:1, :], in_=sscr[0:1, tt * 512:(tt + 1) * 512])
            GD.dma_start(out=rsrow[0:1, :], in_=sscr[1:2, tt * 512:(tt + 1) * 512])
            st_ = GD.inc()
            mk.T.wait(GD, st_)
            ln_apply_tile(mk, xT_in, None, None, tt, lnt, W, 0, xbuf, tmp512, lptok,
                          mu_ap=murow[0:1, :], rs_ap=rsrow[0:1, :])
            return V.inc()

        for tt in range(NTT):
            lt = ln_tt(tt)
            tsl = slice(tt * 512, (tt + 1) * 512)
            Td.wait(V, lt)
            Td.wait(S)
            for kc in range(NKC):
                Td.matmul(mk.pC[0:8, :], gwT[:, kc * 8:(kc + 1) * 8], lnt[:, kc * W:kc * W + 512],
                          start=(kc == 0), stop=(kc == NKC - 1))
            mt = Td.inc()
            S.wait(Td, mt)
            S.wait(Td)
            S.activation(gstage[:, :], mk.pC[0:8, :], AF.Identity, bias=gbc[0:8, :])
            st = S.inc()
            for tc in range(4):
                Td.wait(S)
                Td.wait(V)
                Td.transpose(mk.pG[0:128, 0:8], gstage[0:8, tc * 128:(tc + 1) * 128], mk.ident_f[0:8, 0:8])
                tr = Td.inc()
                V.wait(Td, tr)
                V.tensor_copy(gtok[:, :], mk.pG[0:128, 0:8])
                V.ss()
                V.tensor_reduce(mcol[:, :], gtok[:, :], AX.X, ALU.max)
                V.ss()
                V.tensor_scalar(mcol[:, :], mcol[:, :], -1.0, None, ALU.mult)
                vt = V.inc()
                S.wait(V, vt)
                S.activation(gsc[:, :], gtok[:, :], AF.Exp, bias=mcol[:, :], accum_out=vcol[:, :])
                S.activation(tmpf[:, :], gsc[:, 0:1], AF.Copy)
                st2 = S.inc()
                V.wait(S, st2)
                V.reciprocal(rcol[:, :], vcol[:, :])
                V.ss()
                V.tensor_scalar(gsc[:, :], gsc[:, :], rcol[:, :], None, ALU.mult)
                V.ss()
                V.max(s8[:, :], gsc[:, :])
                V.ss()
                V.tensor_scalar(msk[:, 0:8], gsc[:, :], s8[:, 0:1], None, ALU.is_ge)
                V.tensor_scalar(msk[:, 8:16], gsc[:, :], s8[:, 1:2], None, ALU.is_ge)
                V.ss()
                V.tensor_tensor(msk[:, 8:16], msk[:, 8:16], msk[:, 0:8], ALU.subtract)
                V.tensor_tensor(msk[:, 16:17], s8[:, 0:1], s8[:, 1:2], ALU.add)
                vt = V.inc()
                if dbg is not None and tt == 0 and tc == 0:
                    YD.wait(V, vt)
                    YD.dma_start(out=dbg["gst"], in_=gstage[0:8, :])
                    YD.dma_start(out=dbg["gtok"], in_=gtok[:, :])
                    YD.dma_start(out=dbg["gsc"], in_=gsc[:, :])
                    YD.dma_start(out=dbg["s8"], in_=s8[:, :])
                    YD.dma_start(out=dbg["msk"], in_=msk[:, :])
                    V.wait(YD, YD.cnt)
                    S.wait(YD, YD.cnt)
                    Td.wait(YD, YD.cnt)
                Td.wait(V, vt)
                col = slice(tc * 128, (tc + 1) * 128)
                for src_sl, dst, dd in ((slice(0, 8), m1T, 8), (slice(8, 16), m2T, 8),
                                        (slice(16, 17), valT, 1)):
                    Td.wait(S)
                    Td.wait(YD, YD.cnt)
                    Td.transpose(mk.pG[0:dd, 0:128], msk[:, src_sl], mk.ident_f[:])
                    tr2 = Td.inc()
                    S.wait(Td, tr2)
                    S.activation(dst[0:dd, col], mk.pG[0:dd, 0:128], AF.Copy)
                    S.inc()
            sev3 = S.inc()
            YD.wait(S, sev3)
            YD.dma_start(out=mskscr[0:8, tsl], in_=m1T[0:8, :])
            YD.dma_start(out=mskscr[8:16, tsl], in_=m2T[0:8, :])
            YD.dma_start(out=valscr[0:1, tsl], in_=valT[0:1, :])
        mk.barrier()

        for tt in range(NTT):
            lt = ln_tt(tt)
            tsl = slice(tt * 512, (tt + 1) * 512)
            if tt == 0:
                pass
            GD.wait(V)
            GD.dma_start(out=m1T[0:8, :], in_=mskscr[0:8, tsl])
            GD.dma_start(out=m2T[0:8, :], in_=mskscr[8:16, tsl])
            GD.dma_start(out=valT[0:1, :], in_=valscr[0:1, tsl])
            mtok = GD.inc()
            Td.wait(V, lt)
            Td.wait(S)
            for kc in range(NKC):
                Td.matmul(mk.pA[:, :], AT[:, kc * 128:(kc + 1) * 128], lnt[:, kc * W:kc * W + 512],
                          start=(kc == 0), stop=(kc == NKC - 1))
            lat = Td.inc()
            S.wait(Td, lat)
            S.activation(la_sb[:, :], mk.pA[:, :], AF.Copy)
            st = S.inc()
            for msrc, mdst in ((m1T, mla1), (m2T, mla2)):
                for half in range(2):
                    Td.wait(S)
                    Td.wait(V)
                    Td.wait(GD, mtok)
                    Td.matmul(mk.pG[0:64, :], E8[half][:, :], msrc[0:8, :], start=True, stop=True)
                    xt_ = Td.inc()
                    V.wait(Td, xt_)
                    V.wait(S, st)
                    V.tensor_tensor(mdst[half][0:64, :],
                                    la_sb[half * 64:half * 64 + 64, :], mk.pG[0:64, :], ALU.mult)
                    V.inc()
            mla_tok = V.inc()
            for mf in range(32):
                bps = mk.pC if (mf % 2 == 0) else mk.pD
                Td.wait(S)
                Td.wait(V, mla_tok)
                for kc in range(NKC):
                    Td.matmul(bps[:, :], wiT[:, kc * DFF + mf * 128: kc * DFF + (mf + 1) * 128],
                              lnt[:, kc * W:kc * W + 512], start=(kc == 0), stop=(kc == NKC - 1))
                bt = Td.inc()
                S.wait(Td, bt)
                S.wait(V)
                S.activation(bsb[:, :], bps[:, :], AF.Identity, bias=bic[:, mf:mf + 1])
                sev = S.inc()
                Td.wait(S, sev)
                Td.matmul(mk.pA[:, :], Bst[0][:, mf * 128:(mf + 1) * 128], mla1[0][0:64, :],
                          start=True, stop=False)
                Td.matmul(mk.pA[:, :], Bst[1][:, mf * 128:(mf + 1) * 128], mla1[1][0:64, :],
                          start=False, stop=True)
                Td.matmul(mk.pB[:, :], Bst[0][:, mf * 128:(mf + 1) * 128], mla2[0][0:64, :],
                          start=True, stop=False)
                Td.matmul(mk.pB[:, :], Bst[1][:, mf * 128:(mf + 1) * 128], mla2[1][0:64, :],
                          start=False, stop=True)
                lt2 = Td.inc()
                V.wait(Td, lt2)
                V.wait(S, sev)
                V.scalar_tensor_tensor(r1[:, :], mk.pA[:, :], 1.0, bsb[:, :], ALU.mult, ALU.add)
                V.scalar_tensor_tensor(r2[:, :], mk.pB[:, :], 1.0, bsb[:, :], ALU.mult, ALU.add)
                V.tensor_scalar(r1[:, :], r1[:, :], 0.0, None, ALU.max)
                V.tensor_scalar(r2[:, :], r2[:, :], 0.0, None, ALU.max)
                V.tensor_tensor(hbuf[:, mf * 512:(mf + 1) * 512], r1[:, :], r2[:, :], ALU.add)
                V.inc()
                if dbg is not None and tt == 0 and mf == 0:
                    vv_ = V.inc()
                    YD.wait(V, vv_)
                    YD.dma_start(out=dbg["bsb"], in_=bsb[:, :])
                    YD.dma_start(out=dbg["r1"], in_=r1[:, :])
                    YD.dma_start(out=dbg["mla1"], in_=mla1[0][0:64, :])
                    V.wait(YD, YD.cnt)
                    S.wait(YD, YD.cnt)
                    Td.wait(YD, YD.cnt)
            hv = V.inc()
            for m in range(NKC):
                wg = woT[m % 2]
                if tt == 0:
                    pass
                for mf in range(32):
                    GD.wait(Td)
                    GD.dma_start(out=wstage[:, 0:128],
                                 in_=wo[m * 128:(m + 1) * 128, mf * 128:(mf + 1) * 128])
                    t = GD.inc()
                    Td.wait(GD, t)
                    Td.wait(V)
                    Td.transpose(mk.pT[:, 0:128], wstage[0:128, 0:128], mk.ident[:])
                    tr = Td.inc()
                    V.wait(Td, tr)
                    V.tensor_copy(wg[:, mf * 128:(mf + 1) * 128], mk.pT[:, 0:128])
                    V.inc()
                wv = V.inc()
                ps = mk.pE if (m % 2 == 0) else mk.pF
                Td.wait(V, wv)
                Td.wait(S)
                for mf in range(32):
                    Td.matmul(ps[:, :], wg[:, mf * 128:(mf + 1) * 128],
                              hbuf[:, mf * 512:(mf + 1) * 512], start=(mf == 0), stop=False)
                Td.matmul(ps[:, :], bo_row[0:1, m * 128:(m + 1) * 128], twos[0:1, :],
                          start=False, stop=True)
                pt = Td.inc()
                S.wait(Td, pt)
                S.activation(psb[:, :], ps[:, :], AF.Copy)
                sev2 = S.inc()
                if dbg is not None and tt == 0 and m == 0:
                    YD.wait(S, sev2)
                    YD.dma_start(out=dbg["psb"], in_=psb[:, :])
                    YD.dma_start(out=dbg["hb0"], in_=hbuf[:, 0:512])
                    V.wait(YD, YD.cnt)
                    S.wait(YD, YD.cnt)
                    Td.wait(YD, YD.cnt)
                Td.wait(S)
                Td.wait(GD, mtok)
                Td.matmul(mk.pG[:, :], mk.ones_row_f[0:1, :], valT[0:1, :], start=True, stop=True)
                bv = Td.inc()
                GD.dma_start(out=xrbuf[:, :], in_=xT_in[m * 128:(m + 1) * 128, tsl])
                xt_ = GD.inc()
                V.wait(Td, bv)
                V.wait(S, sev2)
                V.wait(GD, xt_)
                V.scalar_tensor_tensor(nxbuf[:, :], mk.pG[:, :], 1.0, psb[:, :], ALU.mult, ALU.mult)
                V.tensor_tensor(nxbuf[:, :], nxbuf[:, :], xrbuf[:, :], ALU.add)
                nt = V.inc()
                YD.wait(V, nt)
                YD.dma_start(out=xT_out[m * 128:(m + 1) * 128, tsl], in_=nxbuf[:, :])
                V.wait(YD, YD.cnt)
                GD.wait(V, nt)
    mk.barrier()


def emit_ffn1(mk, xT_in, xT_out, w_qkv, b_qkv, w_gate, b_gate, w_exp, b_exp,
              qscr, vscr, kscr, sscr, mskscr, outer_es):
    S, V, Td, G, GD, YD = mk.S, mk.V, mk.T, mk.G, mk.GD, mk.YD
    H = 16
    SCALE = 0.125
    vscr4 = vscr.rearrange("(h p k f) -> h p k f", h=16, p=128, k=32)

    with ExitStack() as les:
        xbuf = mk.sb(mk.name("xbuf"), [128, 1024], F32, les)
        tmp512 = mk.sb(mk.name("tmp512"), [128, 512], F32, les)
        murow = mk.sb(mk.name("murow"), [1, T], F32, les)
        rsrow = mk.sb(mk.name("rsrow"), [1, T], F32, les)
        with ExitStack() as ses:
            sqbuf = mk.sb(mk.name("sqbuf"), [128, 1024], F32, ses)
            tmprow = mk.sb(mk.name("tmprow"), [1, 512], F32, ses)
            ln_stats(mk, xT_in, murow, rsrow, xbuf, sqbuf, tmprow)
            YD.wait(mk.S)
            YD.wait(mk.V)
            YD.dma_start(out=sscr[0:1, :], in_=murow[0:1, :])
            YD.dma_start(out=sscr[1:2, :], in_=rsrow[0:1, :])
            mk.barrier()

        W = 2048
        lnbuf = mk.sb(mk.name("lnbuf"), [128, NKC * W], BF16, les)
        wq = mk.sb(mk.name("wq1f"), [128, NKC * 3072], BF16, les)
        wg = mk.sb(mk.name("wg1f"), [128, NKC * 8], BF16, les)
        bq = mk.sb(mk.name("bq1f"), [128, 24], F32, les)
        bqs = mk.sb(mk.name("bqs1f"), [128, 24], F32, les)
        gbc = mk.sb(mk.name("gbc1"), [8, 1], F32, les)
        qstage = mk.sb(mk.name("qst1f"), [128, 512], BF16, les)
        vstage = mk.sb(mk.name("vst1f"), [128, 512], BF16, les)
        ksub = mk.sb(mk.name("ksub1f"), [128, W], BF16, les)
        vt_tile = mk.sb(mk.name("vt1f"), [128, 128], BF16, les)
        gstage = mk.sb(mk.name("gst1"), [8, 512], F32, les)
        gtok = mk.sb(mk.name("gtok1"), [128, 8], F32, les)
        gsc = mk.sb(mk.name("gsc1"), [128, 8], F32, les)
        mcol = mk.sb(mk.name("mcol1"), [128, 1], F32, les)
        vcol = mk.sb(mk.name("vcol1"), [128, 1], F32, les)
        rcol = mk.sb(mk.name("rcol1"), [128, 1], F32, les)
        tmpf = mk.sb(mk.name("tmpf1"), [128, 1], F32, les)
        s8 = mk.sb(mk.name("s81"), [128, 8], F32, les)
        comb = mk.sb(mk.name("comb1"), [128, 8], F32, les)
        combT = mk.sb(mk.name("combT"), [8, 512], BF16, les)
        for kc in range(NKC):
            GD.dma_start(out=wq[:, kc * 3072:(kc + 1) * 3072], in_=w_qkv[kc * 128:(kc + 1) * 128, :])
            GD.dma_start(out=wg[:, kc * 8:(kc + 1) * 8], in_=w_gate[kc * 128:(kc + 1) * 128, :])
        load_bias_cols(mk, bq, b_qkv, 24)
        GD.dma_start(out=gbc[0:8, :], in_=b_gate.rearrange("(a b) -> a b", b=1))
        wt = GD.inc()
        V.wait(GD, wt)
        V.tensor_scalar(bqs[:, :], bq[:, :], float(SCALE), None, ALU.mult)
        V.inc()
        mk.barrier()

        for half in range(2):
            V.wait(Td)
            ptok = {}
            for i, tt in enumerate(range(half * 4, half * 4 + 4)):
                ln_apply_tile(mk, xT_in, murow, rsrow, tt, lnbuf, W, i * 512, xbuf, tmp512, ptok)
            ln_tok = V.inc()
            for t2 in range(4):
                tt = half * 4 + t2
                tsl = slice(tt * 512, (tt + 1) * 512)
                for m in range(24):
                    ps = mk.pC if (m % 2 == 0) else mk.pD
                    Td.wait(V, ln_tok)
                    Td.wait(S)
                    Td.wait(V)
                    for kc in range(NKC):
                        Td.matmul(ps[:, :], wq[:, kc * 3072 + m * 128: kc * 3072 + (m + 1) * 128],
                                  lnbuf[:, kc * W + t2 * 512: kc * W + t2 * 512 + 512],
                                  start=(kc == 0), stop=(kc == NKC - 1))
                    mt = Td.inc()
                    S.wait(Td, mt)
                    if m < 8:
                        S.wait(YD, YD.cnt)
                        S.activation(qstage[:, :], ps[:, :], AF.Identity, bias=bqs[:, m:m + 1], scale=SCALE)
                        ev = S.inc()
                        YD.wait(S, ev)
                        YD.dma_start(out=qscr[m * 128:(m + 1) * 128, tsl], in_=qstage[:, :])
                        YD.inc()
                    elif m < 16:
                        S.activation(ksub[:, 0:512], ps[:, :], AF.Identity, bias=bq[:, m:m + 1])
                        ev = S.inc()
                        YD.wait(S, ev)
                        YD.dma_start(out=kscr[(m - 8) * 128:(m - 7) * 128, tsl], in_=ksub[:, 0:512])
                        kd = YD.inc()
                        S.wait(YD, kd)
                    else:
                        S.wait(Td)
                        S.activation(vstage[:, :], ps[:, :], AF.Identity, bias=bq[:, m:m + 1])
                        ev = S.inc()
                        c = m - 16
                        for tc in range(4):
                            kchunk = tt * 4 + tc
                            Td.wait(S, ev)
                            Td.wait(V)
                            Td.transpose(mk.pT[:, 0:128], vstage[:, tc * 128:(tc + 1) * 128], mk.ident[:])
                            ttr = Td.inc()
                            V.wait(Td, ttr)
                            V.tensor_copy(vt_tile[:, :], mk.pT[:, 0:128])
                            vtt = V.inc()
                            GD.wait(V, vtt)
                            GD.dma_start(out=vscr4[2 * c, :, kchunk, :], in_=vt_tile[:, 0:64])
                            GD.dma_start(out=vscr4[2 * c + 1, :, kchunk, :], in_=vt_tile[:, 64:128])
                            V.wait(GD, GD.cnt)
                Td.wait(V, ln_tok)
                Td.wait(S)
                for kc in range(NKC):
                    Td.matmul(mk.pC[0:8, :], wg[:, kc * 8:(kc + 1) * 8],
                              lnbuf[:, kc * W + t2 * 512: kc * W + t2 * 512 + 512],
                              start=(kc == 0), stop=(kc == NKC - 1))
                mt = Td.inc()
                S.wait(Td, mt)
                S.wait(Td)
                S.activation(gstage[:, :], mk.pC[0:8, :], AF.Identity, bias=gbc[0:8, :])
                st = S.inc()
                for tc in range(4):
                    Td.wait(S)
                    Td.wait(V)
                    Td.transpose(mk.pG[0:128, 0:8], gstage[0:8, tc * 128:(tc + 1) * 128],
                                 mk.ident_f[0:8, 0:8])
                    tr = Td.inc()
                    V.wait(Td, tr)
                    V.tensor_copy(gtok[:, :], mk.pG[0:128, 0:8])
                    V.ss()
                    V.tensor_reduce(mcol[:, :], gtok[:, :], AX.X, ALU.max)
                    V.ss()
                    V.tensor_scalar(mcol[:, :], mcol[:, :], -1.0, None, ALU.mult)
                    vt = V.inc()
                    S.wait(V, vt)
                    S.activation(gsc[:, :], gtok[:, :], AF.Exp, bias=mcol[:, :], accum_out=vcol[:, :])
                    S.activation(tmpf[:, :], gsc[:, 0:1], AF.Copy)
                    st2 = S.inc()
                    V.wait(S, st2)
                    V.reciprocal(rcol[:, :], vcol[:, :])
                    V.ss()
                    V.tensor_scalar(gsc[:, :], gsc[:, :], rcol[:, :], None, ALU.mult)
                    V.ss()
                    V.max(s8[:, :], gsc[:, :])
                    V.ss()
                    V.tensor_scalar(comb[:, :], gsc[:, :], s8[:, 1:2], None, ALU.is_ge)
                    V.ss()
                    V.tensor_tensor(comb[:, :], comb[:, :], gsc[:, :], ALU.mult)
                    vt = V.inc()
                    Td.wait(V, vt)
                    Td.wait(S)
                    Td.transpose(mk.pG[0:8, 0:128], comb[:, :], mk.ident_f[:])
                    tr2 = Td.inc()
                    S.wait(Td, tr2)
                    S.activation(combT[0:8, tc * 128:(tc + 1) * 128], mk.pG[0:8, 0:128], AF.Copy)
                    S.inc()
                sev = S.inc()
                YD.wait(S, sev)
                YD.dma_start(out=mskscr[0:8, tsl], in_=combT[0:8, :])
                YD.inc()
        mk.barrier()

    sabuf = [mk.sb(mk.name("sab1f"), [128, T], BF16, outer_es) for _ in range(NKC)]
    with ExitStack() as les:
        kres = [mk.sb(mk.name("kres1f"), [128, T], BF16, les) for _ in range(8)]
        for i in range(8):
            GD.dma_start(out=kres[i][:, :], in_=kscr[i * 128:(i + 1) * 128, :])
        GD.inc()
        mk.barrier()
        negc, biascols, _m = compute_chead(
            mk, lambda h: qscr[h * 64:(h + 1) * 64, :], kres, 16, les,
            krow_of=lambda h: (kres[h // 2], (h % 2) * 64))
        cfg = dict(H=16, causal=False, naug=0,
                   q_slice=lambda h, qt: qscr[h * 64:(h + 1) * 64, qt * 512:(qt + 1) * 512],
                   qrow=lambda h: (h % 2) * 64,
                   k_lhsT=lambda h, kc: kres[h // 2][(h % 2) * 64:(h % 2) * 64 + 64, kc * 128:(kc + 1) * 128],
                   biascol=lambda h, kc: biascols[:, h:h + 1],
                   vscr4=vscr.rearrange("(h p k f) -> h p k f", h=16, p=128, k=32),
                   kv=lambda h: h,
                   nk=lambda qt: 32,
                   ynrm_dest=lambda h, qt: sabuf[h // 2][(h % 2) * 64:(h % 2) * 64 + 64,
                                                         qt * 512:(qt + 1) * 512],
                   sa_write=None,
                   after_qt=lambda qt: None)
        emit_softmax_attention(mk, cfg, les)

    with ExitStack() as les:
        xbuf = mk.sb(mk.name("xbufm"), [128, 1024], F32, les)
        tmp512 = mk.sb(mk.name("tmp512m"), [128, 512], F32, les)
        murow = mk.sb(mk.name("mu1f"), [1, 512], F32, les)
        rsrow = mk.sb(mk.name("rs1f"), [1, 512], F32, les)
        lnt = mk.sb(mk.name("lnt1f"), [128, NKC * 512], BF16, les)
        lsc = mk.sb(mk.name("lsc1f"), [128, NKC * 512], BF16, les)
        combT = mk.sb(mk.name("combT2"), [8, 512], BF16, les)
        combrow = mk.sb(mk.name("combrow"), [1, 8 * 512], F32, les)
        wexp = [mk.sb(mk.name("wexp"), [128, NKC * 1024], BF16, les) for _ in range(2)]
        acc = mk.sb(mk.name("accm"), [128, NKC * 512], F32, les)
        bexp = mk.sb(mk.name("bexp"), [8, 1024], BF16, les)
        psb = mk.sb(mk.name("psbm"), [128, 512], F32, les)
        xrbuf = mk.sb(mk.name("xrm"), [128, 512], F32, les)
        nxbuf = mk.sb(mk.name("nxm"), [128, 512], F32, les)
        GD.dma_start(out=bexp[0:8, :], in_=b_exp[:, :])
        wl_tok = [0, 0]
        for kc in range(NKC):
            GD.dma_start(out=wexp[0][:, kc * 1024:(kc + 1) * 1024],
                         in_=w_exp[0, kc * 128:(kc + 1) * 128, :])
        wl_tok[0] = GD.inc()
        mk.barrier()

        lptok = {}
        for tt in range(NTT):
            tsl = slice(tt * 512, (tt + 1) * 512)
            GD.wait(V)
            GD.dma_start(out=murow[0:1, :], in_=sscr[0:1, tsl])
            GD.dma_start(out=rsrow[0:1, :], in_=sscr[1:2, tsl])
            GD.dma_start(out=combT[0:8, :], in_=mskscr[0:8, tsl])
            st_ = GD.inc()
            Td.wait(GD, st_)
            V.wait(Td)
            ln_apply_tile(mk, xT_in, None, None, tt, lnt, 512, 0, xbuf, tmp512, lptok,
                          mu_ap=murow[0:1, :], rs_ap=rsrow[0:1, :])
            lt = V.inc()
            for e in range(8):
                Td.wait(GD, st_)
                Td.wait(S)
                Td.matmul(mk.pG[0:1, 0:512], mk.ident[0:8, e:e + 1],
                          combT[0:8, :], start=True, stop=True)
                tr = Td.inc()
                S.wait(Td, tr)
                S.activation(combrow[0:1, e * 512:(e + 1) * 512], mk.pG[0:1, 0:512], AF.Copy)
                S.inc()
            crt = S.inc()
            for e in range(8):
                ne = e + 1
                if ne < 8 or tt + 1 < NTT:
                    nb = ne % 2
                    GD.wait(Td)
                    for kc in range(NKC):
                        GD.dma_start(out=wexp[nb][:, kc * 1024:(kc + 1) * 1024],
                                     in_=w_exp[ne % 8, kc * 128:(kc + 1) * 128, :])
                    wl_tok[nb] = GD.inc()
                Td.wait(V)
                Td.wait(S, crt)
                Td.matmul(mk.pG[:, :], mk.ones_row_f[0:1, :], combrow[0:1, e * 512:(e + 1) * 512],
                          start=True, stop=True)
                bt = Td.inc()
                V.wait(Td, bt)
                for kc in range(NKC):
                    V.tensor_tensor(lsc[:, kc * 512:(kc + 1) * 512], lnt[:, kc * 512:(kc + 1) * 512],
                                    mk.pG[:, :], ALU.mult)
                sc_tok = V.inc()
                eb = e % 2
                Td.wait(GD, wl_tok[eb])
                Td.wait(V, sc_tok)
                for m in range(NKC):
                    ps2 = mk.pC if (m % 2 == 0) else mk.pD
                    Td.wait(V)
                    for kc in range(NKC):
                        Td.matmul(ps2[:, :], wexp[eb][:, kc * 1024 + m * 128: kc * 1024 + (m + 1) * 128],
                                  lsc[:, kc * 512:(kc + 1) * 512],
                                  start=(kc == 0), stop=(kc == NKC - 1))
                    emt = Td.inc()
                    V.wait(Td, emt)
                    if e == 0:
                        V.tensor_copy(acc[:, m * 512:(m + 1) * 512], ps2[:, :])
                    else:
                        V.scalar_tensor_tensor(acc[:, m * 512:(m + 1) * 512], ps2[:, :], 1.0,
                                               acc[:, m * 512:(m + 1) * 512], ALU.mult, ALU.add)
                    V.inc()
            acc_tok = V.inc()
            for m in range(NKC):
                ps2 = mk.pE if (m % 2 == 0) else mk.pF
                Td.wait(GD, st_)
                Td.wait(V)
                Td.matmul(ps2[:, :], bexp[0:8, m * 128:(m + 1) * 128], combT[0:8, :],
                          start=True, stop=True)
                pt = Td.inc()
                GD.dma_start(out=xrbuf[:, :], in_=xT_in[m * 128:(m + 1) * 128, tsl])
                xt_ = GD.inc()
                V.wait(Td, pt)
                V.wait(GD, xt_)
                V.wait(V, acc_tok)
                V.scalar_tensor_tensor(nxbuf[:, :], ps2[:, :], 1.0, acc[:, m * 512:(m + 1) * 512],
                                       ALU.mult, ALU.add)
                V.tensor_tensor(nxbuf[:, :], nxbuf[:, :], xrbuf[:, :], ALU.add)
                V.tensor_tensor(nxbuf[:, :], nxbuf[:, :], sabuf[m][:, tsl], ALU.add)
                nt = V.inc()
                YD.wait(V, nt)
                YD.dma_start(out=xT_out[m * 128:(m + 1) * 128, tsl], in_=nxbuf[:, :])
                V.wait(YD, YD.cnt)
                GD.wait(V, nt)
    mk.barrier()


def emit_ffn2(mk, xT_in, xT_out, cw, cb, gw, gb, ar_ap, outer_es):
    S, V, Td, G, GD, YD = mk.S, mk.V, mk.T, mk.G, mk.GD, mk.YD
    with ExitStack() as les:
        murow = mk.sb(mk.name("murow"), [1, T], F32, les)
        rsrow = mk.sb(mk.name("rsrow"), [1, T], F32, les)
        xbuf = mk.sb(mk.name("xbuf"), [128, 1024], F32, les)
        sqbuf = mk.sb(mk.name("sqbuf"), [128, 1024], F32, les)
        tmprow = mk.sb(mk.name("tmprow"), [1, 512], F32, les)
        tmp512 = mk.sb(mk.name("tmp512"), [128, 512], F32, les)
        ln_stats(mk, xT_in, murow, rsrow, xbuf, sqbuf, tmprow)

        w1 = mk.sb(mk.name("cw1"), [128, NKC * 1024], BF16, les)
        w2 = mk.sb(mk.name("cw2"), [128, NKC * 512], BF16, les)
        w3 = mk.sb(mk.name("cw3"), [128, 4 * 256], BF16, les)
        w4 = mk.sb(mk.name("cw4"), [128, 2], BF16, les)
        g1 = mk.sb(mk.name("gw1"), [128, NKC * 512], BF16, les)
        g2 = mk.sb(mk.name("gw2"), [128, 4 * 256], BF16, les)
        g3 = mk.sb(mk.name("gw3"), [128, 2], BF16, les)
        b1c = mk.sb(mk.name("cb1"), [128, 8], F32, les)
        b2c = mk.sb(mk.name("cb2"), [128, 4], F32, les)
        b3c = mk.sb(mk.name("cb3"), [128, 2], F32, les)
        b4c = mk.sb(mk.name("cb4"), [1, 1], F32, les)
        gb1c = mk.sb(mk.name("gb1"), [128, 4], F32, les)
        gb2c = mk.sb(mk.name("gb2"), [128, 2], F32, les)
        gb3c = mk.sb(mk.name("gb3"), [1, 1], F32, les)
        arw = mk.sb(mk.name("arw"), [1, 2], F32, les)
        gcac = mk.sb(mk.name("gcac"), [128, 8], F32, les)
        gcv = mk.sb(mk.name("gcv"), [128, 8], F32, les)
        O = mk.sb(mk.name("O2"), [128, NKC * 512], F32, les)
        obf = mk.sb(mk.name("obf2"), [128, NKC * 512], BF16, les)
        h1b = mk.sb(mk.name("h1b"), [128, NKC * 512], BF16, les)
        h2b = mk.sb(mk.name("h2b"), [128, 4 * 512], BF16, les)
        h3b = mk.sb(mk.name("h3b"), [128, 2 * 512], BF16, les)
        tfr = mk.sb(mk.name("tfr"), [1, 512], F32, les)
        cplr = mk.sb(mk.name("cplr"), [1, 512], F32, les)
        tA = mk.sb(mk.name("tA2"), [128, 512], F32, les)
        tB = mk.sb(mk.name("tB2"), [128, 512], F32, les)
        xrbuf = mk.sb(mk.name("xr2"), [128, 512], F32, les)
        nxbuf = mk.sb(mk.name("nx2"), [128, 512], F32, les)

        for kc in range(NKC):
            GD.dma_start(out=w1[:, kc * 1024:(kc + 1) * 1024], in_=cw[0][kc * 128:(kc + 1) * 128, :])
            GD.dma_start(out=w2[:, kc * 512:(kc + 1) * 512], in_=cw[1][kc * 128:(kc + 1) * 128, :])
            GD.dma_start(out=g1[:, kc * 512:(kc + 1) * 512], in_=gw[0][kc * 128:(kc + 1) * 128, :])
        for kc in range(4):
            GD.dma_start(out=w3[:, kc * 256:(kc + 1) * 256], in_=cw[2][kc * 128:(kc + 1) * 128, :])
            GD.dma_start(out=g2[:, kc * 256:(kc + 1) * 256], in_=gw[1][kc * 128:(kc + 1) * 128, :])
        for kc in range(2):
            GD.dma_start(out=w4[:, kc:kc + 1], in_=cw[3][kc * 128:(kc + 1) * 128, :])
            GD.dma_start(out=g3[:, kc:kc + 1], in_=gw[2][kc * 128:(kc + 1) * 128, :])
        load_bias_cols(mk, b1c, cb[0], 8)
        load_bias_cols(mk, b2c, cb[1], 4)
        load_bias_cols(mk, b3c, cb[2], 2)
        GD.dma_start(out=b4c[0:1, :], in_=cb[3].rearrange("(a b) -> a b", b=1))
        load_bias_cols(mk, gb1c, gb[0], 4)
        load_bias_cols(mk, gb2c, gb[1], 2)
        GD.dma_start(out=gb3c[0:1, :], in_=gb[2].rearrange("(a b) -> a b", b=1))
        GD.dma_start(out=arw[0:1, 0:1], in_=ar_ap.rearrange("(a b) -> a b", b=1))
        wt = GD.inc()
        V.wait(GD, wt)
        V.tensor_scalar(arw[0:1, 0:1], arw[0:1, 0:1], 2.0, None, ALU.mult)
        V.inc()
        mk.barrier()

        lptok = {}
        for tt in range(NTT):
            V.wait(Td)
            ln_apply_tile(mk, xT_in, murow, rsrow, tt, obf, 512, 0, xbuf, tmp512, lptok)
            lt = V.inc()
            for kc in range(NKC):
                if tt == 0:
                    V.tensor_reduce(gcac[:, kc:kc + 1], obf[:, kc * 512:(kc + 1) * 512], AX.X, ALU.add)
                else:
                    V.tensor_reduce(gcv[:, kc:kc + 1], obf[:, kc * 512:(kc + 1) * 512], AX.X, ALU.add)
                    V.ss()
                    V.tensor_tensor(gcac[:, kc:kc + 1], gcac[:, kc:kc + 1], gcv[:, kc:kc + 1], ALU.add)
                    V.ss()
            V.inc()
        V.tensor_scalar(gcac[:, :], gcac[:, :], 1.0 / T, None, ALU.mult)
        gct = V.inc()
        V.ss()
        gcb = gcv
        gcbf = mk.sb(mk.name("gcbf"), [128, 8], BF16, les)
        V.tensor_copy(gcbf[:, :], gcac[:, :])
        gct = V.inc()
        for m in range(4):
            Td.wait(V, gct)
            Td.wait(S)
            for kc in range(NKC):
                Td.matmul(mk.pC[:, 0:1], g1[:, kc * 512 + m * 128: kc * 512 + m * 128 + 128],
                          gcbf[:, kc:kc + 1], start=(kc == 0), stop=(kc == NKC - 1))
            mt = Td.inc()
            S.wait(Td, mt)
            S.activation(gcv[:, m:m + 1], mk.pC[:, 0:1], AF.Gelu, bias=gb1c[:, m:m + 1])
            S.inc()
        st = S.inc()
        V.wait(S, st)
        V.tensor_copy(gcbf[:, 0:4], gcv[:, 0:4])
        gct = V.inc()
        for m in range(2):
            Td.wait(V, gct)
            Td.wait(S)
            for kc in range(4):
                Td.matmul(mk.pC[:, 0:1], g2[:, kc * 256 + m * 128: kc * 256 + m * 128 + 128],
                          gcbf[:, kc:kc + 1], start=(kc == 0), stop=(kc == 3))
            mt = Td.inc()
            S.wait(Td, mt)
            S.activation(gcv[:, m:m + 1], mk.pC[:, 0:1], AF.Gelu, bias=gb2c[:, m:m + 1])
            S.inc()
        st = S.inc()
        V.wait(S, st)
        V.tensor_copy(gcbf[:, 0:2], gcv[:, 0:2])
        gct = V.inc()
        Td.wait(V, gct)
        Td.wait(S)
        for kc in range(2):
            Td.matmul(mk.pC[0:1, 0:1], g3[:, kc:kc + 1], gcbf[:, kc:kc + 1],
                      start=(kc == 0), stop=(kc == 1))
        mt = Td.inc()
        S.wait(Td, mt)
        S.activation(arw[0:1, 1:2], mk.pC[0:1, 0:1], AF.Sigmoid, bias=gb3c[0:1, :])
        S.ss()
        S.activation(arw[0:1, 1:2], arw[0:1, 1:2], AF.Copy, scale=0.7)
        S.inc()
        mk.barrier()

        lptok2 = {}
        for tt in range(NTT):
            tsl = slice(tt * 512, (tt + 1) * 512)
            V.wait(Td)
            bt = ln_bcast(mk, murow[0:1, tsl], rsrow[0:1, tsl])
            for kc in range(NKC):
                hh = kc % 2
                buf = xbuf[:, hh * 512:hh * 512 + 512]
                if lptok2.get(("x", hh)):
                    GD.wait(V, lptok2[("x", hh)])
                GD.dma_start(out=buf, in_=xT_in[kc * 128:(kc + 1) * 128, tsl])
                dt = GD.inc()
                V.wait(GD, dt)
                V.wait(Td, bt)
                V.tensor_tensor(tmp512[:, :], buf, mk.pE[:, :], ALU.subtract)
                V.tensor_tensor(O[:, kc * 512:(kc + 1) * 512], tmp512[:, :], mk.pF[:, :], ALU.mult)
                lptok2[("x", hh)] = V.inc()
            for cyc in range(3):
                V.wait(Td)
                for kc in range(NKC):
                    V.tensor_copy(obf[:, kc * 512:(kc + 1) * 512], O[:, kc * 512:(kc + 1) * 512])
                ot = V.inc()
                for m in range(NKC):
                    ps = mk.pC if (m % 2 == 0) else mk.pD
                    Td.wait(V, ot)
                    Td.wait(S)
                    for kc in range(NKC):
                        Td.matmul(ps[:, :], w1[:, kc * 1024 + m * 128: kc * 1024 + (m + 1) * 128],
                                  obf[:, kc * 512:(kc + 1) * 512], start=(kc == 0), stop=(kc == NKC - 1))
                    mt = Td.inc()
                    S.wait(Td, mt)
                    S.activation(h1b[:, m * 512:(m + 1) * 512], ps[:, :], AF.Gelu, bias=b1c[:, m:m + 1])
                    S.inc()
                s1 = S.inc()
                for m in range(4):
                    ps = mk.pC if (m % 2 == 0) else mk.pD
                    Td.wait(S)
                    for kc in range(NKC):
                        Td.matmul(ps[:, :], w2[:, kc * 512 + m * 128: kc * 512 + (m + 1) * 128],
                                  h1b[:, kc * 512:(kc + 1) * 512], start=(kc == 0), stop=(kc == NKC - 1))
                    mt = Td.inc()
                    S.wait(Td, mt)
                    S.activation(h2b[:, m * 512:(m + 1) * 512], ps[:, :], AF.Gelu, bias=b2c[:, m:m + 1])
                    S.inc()
                s2 = S.inc()
                for m in range(2):
                    ps = mk.pC if (m % 2 == 0) else mk.pD
                    Td.wait(S)
                    for kc in range(4):
                        Td.matmul(ps[:, :], w3[:, kc * 256 + m * 128: kc * 256 + (m + 1) * 128],
                                  h2b[:, kc * 512:(kc + 1) * 512], start=(kc == 0), stop=(kc == 3))
                    mt = Td.inc()
                    S.wait(Td, mt)
                    S.activation(h3b[:, m * 512:(m + 1) * 512], ps[:, :], AF.Gelu, bias=b3c[:, m:m + 1])
                    S.inc()
                s3 = S.inc()
                Td.wait(S, s3)
                for kc in range(2):
                    Td.matmul(mk.pC[0:1, :], w4[:, kc:kc + 1], h3b[:, kc * 512:(kc + 1) * 512],
                              start=(kc == 0), stop=(kc == 1))
                mt = Td.inc()
                S.wait(Td, mt)
                S.activation(tfr[0:1, :], mk.pC[0:1, :], AF.Sigmoid, bias=b4c[0:1, :])
                s4 = S.inc()
                V.wait(S, s4)
                V.tensor_scalar(cplr[0:1, :], tfr[0:1, :], 0.3, arw[0:1, 1:2], ALU.mult, ALU.add)
                V.ss()
                V.tensor_scalar(cplr[0:1, :], cplr[0:1, :], -0.5, None, ALU.add)
                V.ss()
                V.tensor_scalar(cplr[0:1, :], cplr[0:1, :], arw[0:1, 0:1], 0.1, ALU.mult, ALU.add)
                cpt = V.inc()
                Td.wait(V, cpt)
                Td.matmul(mk.pG[:, :], mk.ones_row_f[0:1, :], cplr[0:1, :], start=True, stop=True)
                cbt = Td.inc()
                V.wait(Td, cbt)
                for kc in range(4):
                    sA = 1.0 if kc < 2 else -1.0
                    sB = -1.0 if kc < 2 else 1.0
                    a_sl = slice(kc * 512, (kc + 1) * 512)
                    b_sl = slice((kc + 4) * 512, (kc + 5) * 512)
                    V.scalar_tensor_tensor(tA[:, :], mk.pG[:, :], sA, O[:, b_sl], ALU.mult, ALU.mult)
                    V.scalar_tensor_tensor(tB[:, :], mk.pG[:, :], sB, O[:, a_sl], ALU.mult, ALU.mult)
                    V.tensor_tensor(O[:, a_sl], O[:, a_sl], tA[:, :], ALU.add)
                    V.tensor_tensor(O[:, b_sl], O[:, b_sl], tB[:, :], ALU.add)
                V.inc()
            for m in range(NKC):
                hh = m % 2
                buf = xbuf[:, hh * 512:hh * 512 + 512]
                if lptok2.get(("x", hh)):
                    GD.wait(V, lptok2[("x", hh)])
                GD.dma_start(out=buf, in_=xT_in[m * 128:(m + 1) * 128, tsl])
                dt = GD.inc()
                V.wait(GD, dt)
                V.tensor_tensor(nxbuf[:, :], buf, O[:, m * 512:(m + 1) * 512], ALU.add)
                lptok2[("x", hh)] = V.inc()
                YD.wait(V, lptok2[("x", hh)])
                YD.dma_start(out=xT_out[m * 128:(m + 1) * 128, tsl], in_=nxbuf[:, :])
                V.wait(YD, YD.cnt)
    mk.barrier()


from concourse.bass_utils import run_bass_kernel_spmd

QSCR_SH = [1088, 4096]
VSCR_SH = [16 * 32 * 128 * 64]


def build_fused():
    nc = bass.Bass("TRN2", target_bir_lowering=False)
    xT = nc.dram_tensor("xT", [1024, 4096], F32, kind="ExternalInput").ap()
    xTo = nc.dram_tensor("xTo", [1024, 4096], F32, kind="ExternalOutput").ap()
    xA = nc.dram_tensor("xA", [1024, 4096], F32, kind="Internal").ap()
    xB = nc.dram_tensor("xB", [1024, 4096], F32, kind="Internal").ap()
    qscr = nc.dram_tensor("qscr", QSCR_SH, BF16, kind="Internal").ap()
    vscr = nc.dram_tensor("vscr", VSCR_SH, BF16, kind="Internal").ap()
    kscr = nc.dram_tensor("kscr", [1024, 4096], BF16, kind="Internal").ap()
    sscr = nc.dram_tensor("sscr", [2, 4096], F32, kind="Internal").ap()
    mskscr = nc.dram_tensor("mskscr", [16, 4096], BF16, kind="Internal").ap()
    valscr = nc.dram_tensor("valscr", [1, 4096], F32, kind="Internal").ap()

    W = {}
    specs = {
        "L0_attn_W_attn": [1024, 3072], "L0_attn_W_proj": [1024, 1024],
        "L0_ffn_Win": [1024, 64], "L0_ffn_bin_": [64], "L0_ffn_W1": [64, 16],
        "L0_ffn_b1": [64, 16], "L0_ffn_W2": [64, 16], "L0_ffn_b2": [64],
        "L0_ffn_Wout": [64, 1024], "L0_ffn_bout": [1024],
        "L1_attn_W_qkv": [1024, 3072], "L1_attn_b_qkv": [3072],
        "L1_attn_W_out": [1024, 1024], "L1_attn_b_out": [1024],
        "L1_ffn_W_qkv": [1024, 3072], "L1_ffn_b_qkv": [3072],
        "L1_ffn_W_gate": [1024, 8], "L1_ffn_b_gate": [8],
        "L1_ffn_W_experts": [8, 1024, 1024], "L1_ffn_b_experts": [8, 1024],
        "L2_attn_Wq": [1024, 1024], "L2_attn_bq": [1024], "L2_attn_Wk": [1024, 1024],
        "L2_attn_bk": [1024], "L2_attn_Wv": [1024, 1024], "L2_attn_bv": [1024],
        "L2_attn_Wo": [1024, 1024], "L2_attn_bo": [1024],
        "L2_ffn_cn_w1": [1024, 1024], "L2_ffn_cn_b1": [1024],
        "L2_ffn_cn_w2": [1024, 512], "L2_ffn_cn_b2": [512],
        "L2_ffn_cn_w3": [512, 256], "L2_ffn_cn_b3": [256],
        "L2_ffn_cn_w4": [256, 1], "L2_ffn_cn_b4": [1],
        "L2_ffn_gc_w1": [1024, 512], "L2_ffn_gc_b1": [512],
        "L2_ffn_gc_w2": [512, 256], "L2_ffn_gc_b2": [256],
        "L2_ffn_gc_w3": [256, 1], "L2_ffn_gc_b3": [1],
        "L2_ffn_adaptive_range": [1],
        "L3_attn_Wq": [1024, 1024], "L3_attn_bq": [1024], "L3_attn_Wk": [1024, 256],
        "L3_attn_bk": [256], "L3_attn_Wv": [1024, 256], "L3_attn_bv": [256],
        "L3_attn_Wo": [1024, 1024], "L3_attn_bo": [1024], "L3_attn_slopes": [16],
        "L3_ffn_wi": [4096, 1024], "L3_ffn_bi": [4096], "L3_ffn_wo": [1024, 4096],
        "L3_ffn_bo": [1024], "L3_ffn_lora_A": [8, 16, 1024], "L3_ffn_lora_B": [8, 4096, 16],
        "L3_ffn_gate_w": [8, 1024], "L3_ffn_gate_b": [8],
    }
    for k, sh in specs.items():
        W[k] = nc.dram_tensor(k, sh, F32, kind="ExternalInput").ap()

    with ExitStack() as es:
        mk = MK(nc, es)
        mk.init_consts()
        mk.barrier()
        stages = [xT, xA, xB, xA, xB, xA, xB, xA, xTo]
        with ExitStack() as oes:
            kres = [mk.sb(mk.name("kres"), [128, T], BF16, oes) for _ in range(8)]
            with ExitStack() as les:
                st = emit_attn0(mk, stages[0], stages[1], W["L0_attn_W_attn"],
                                W["L0_attn_W_proj"], qscr, vscr, kres, les)
            with ExitStack() as les:
                emit_attn0_attention(mk, stages[0], stages[1], W["L0_attn_W_proj"],
                                     qscr, kres, st["vscr4"], les)
        with ExitStack() as oes:
            emit_ffn0(mk, stages[1], stages[2], W["L0_ffn_Win"], W["L0_ffn_bin_"],
                      W["L0_ffn_W1"], W["L0_ffn_b1"], W["L0_ffn_W2"], W["L0_ffn_b2"],
                      W["L0_ffn_Wout"], W["L0_ffn_bout"], oes)
        with ExitStack() as oes:
            emit_attn1(mk, stages[2], stages[3], W["L1_attn_W_qkv"], W["L1_attn_b_qkv"],
                       W["L1_attn_W_out"], W["L1_attn_b_out"], qscr, vscr, kscr, oes)
        with ExitStack() as oes:
            emit_ffn1(mk, stages[3], stages[4], W["L1_ffn_W_qkv"], W["L1_ffn_b_qkv"],
                      W["L1_ffn_W_gate"], W["L1_ffn_b_gate"], W["L1_ffn_W_experts"],
                      W["L1_ffn_b_experts"], qscr, vscr, kscr, sscr, mskscr, oes)
        with ExitStack() as oes:
            emit_attn2(mk, stages[4], stages[5], W["L2_attn_Wq"], W["L2_attn_bq"],
                       W["L2_attn_Wk"], W["L2_attn_bk"], W["L2_attn_Wv"], W["L2_attn_bv"],
                       W["L2_attn_Wo"], W["L2_attn_bo"], qscr, vscr, oes)
        with ExitStack() as oes:
            cwt = [W["L2_ffn_cn_w1"], W["L2_ffn_cn_w2"], W["L2_ffn_cn_w3"], W["L2_ffn_cn_w4"]]
            cbt = [W["L2_ffn_cn_b1"], W["L2_ffn_cn_b2"], W["L2_ffn_cn_b3"], W["L2_ffn_cn_b4"]]
            gwt = [W["L2_ffn_gc_w1"], W["L2_ffn_gc_w2"], W["L2_ffn_gc_w3"]]
            gbt = [W["L2_ffn_gc_b1"], W["L2_ffn_gc_b2"], W["L2_ffn_gc_b3"]]
            emit_ffn2(mk, stages[5], stages[6], cwt, cbt, gwt, gbt,
                      W["L2_ffn_adaptive_range"], oes)
        with ExitStack() as oes:
            emit_attn3(mk, stages[6], stages[7], W["L3_attn_Wq"], W["L3_attn_bq"],
                       W["L3_attn_Wk"], W["L3_attn_bk"], W["L3_attn_Wv"], W["L3_attn_bv"],
                       W["L3_attn_Wo"], W["L3_attn_bo"], W["L3_attn_slopes"],
                       qscr, vscr, oes)
        with ExitStack() as oes:
            emit_ffn3(mk, stages[7], stages[8], W["L3_ffn_wi"], W["L3_ffn_bi"],
                      W["L3_ffn_wo"], W["L3_ffn_bo"], W["L3_ffn_lora_A"], W["L3_ffn_lora_B"],
                      W["L3_ffn_gate_w"], W["L3_ffn_gate_b"], sscr, mskscr, valscr, oes)
        mk.barrier()
    return nc


_ALL_WKEYS = [
    "L0_attn_W_attn", "L0_attn_W_proj",
    "L0_ffn_Win", "L0_ffn_bin_", "L0_ffn_W1", "L0_ffn_b1", "L0_ffn_W2", "L0_ffn_b2",
    "L0_ffn_Wout", "L0_ffn_bout",
    "L1_attn_W_qkv", "L1_attn_b_qkv", "L1_attn_W_out", "L1_attn_b_out",
    "L1_ffn_W_qkv", "L1_ffn_b_qkv", "L1_ffn_W_gate", "L1_ffn_b_gate",
    "L1_ffn_W_experts", "L1_ffn_b_experts",
    "L2_attn_Wq", "L2_attn_bq", "L2_attn_Wk", "L2_attn_bk", "L2_attn_Wv", "L2_attn_bv",
    "L2_attn_Wo", "L2_attn_bo",
    "L2_ffn_cn_w1", "L2_ffn_cn_b1", "L2_ffn_cn_w2", "L2_ffn_cn_b2", "L2_ffn_cn_w3",
    "L2_ffn_cn_b3", "L2_ffn_cn_w4", "L2_ffn_cn_b4", "L2_ffn_gc_w1", "L2_ffn_gc_b1",
    "L2_ffn_gc_w2", "L2_ffn_gc_b2", "L2_ffn_gc_w3", "L2_ffn_gc_b3",
    "L2_ffn_adaptive_range",
    "L3_attn_Wq", "L3_attn_bq", "L3_attn_Wk", "L3_attn_bk", "L3_attn_Wv", "L3_attn_bv",
    "L3_attn_Wo", "L3_attn_bo", "L3_attn_slopes",
    "L3_ffn_wi", "L3_ffn_bi", "L3_ffn_wo", "L3_ffn_bo", "L3_ffn_lora_A", "L3_ffn_lora_B",
    "L3_ffn_gate_w", "L3_ffn_gate_b",
]


def kernel(**inputs):
    x = inputs["L0_attn_x"]
    ncore = 8
    nc = build_fused()
    wmap = {k: np.ascontiguousarray(inputs[k], dtype=np.float32) for k in _ALL_WKEYS}
    in_maps = [{"xT": np.ascontiguousarray(x[i].T).astype(np.float32, copy=False), **wmap}
               for i in range(ncore)]
    res = run_bass_kernel_spmd(nc, in_maps, core_ids=list(range(ncore)))
    out = np.stack([res.results[i]["xTo"].T for i in range(ncore)], axis=0).astype(np.float32)
    return out
